# version 06 (final)
# speedup vs baseline: 1.3592x; 1.0348x over previous
import re
import numpy as np
import ml_dtypes
import concourse.bass as bass
import concourse.mybir as mybir
from concourse.bass_utils import run_bass_kernel_spmd

F32 = mybir.dt.float32
BF16 = mybir.dt.bfloat16
AF = mybir.ActivationFunctionType
ALU = mybir.AluOpType
AX = mybir.AxisListType.X
bfdt = ml_dtypes.bfloat16

D = 1024
SEQ = 2048
NSEQ = 2
T = NSEQ * SEQ
DC = 8
TT = 512
NTT = T // TT
TPS = NTT // NSEQ
NCORES = 8
LN_EPS = 1e-5


class Ctr:
    def __init__(self, nc, name):
        self.sem = nc.alloc_semaphore(name)
        self.n = 0

    def inc(self, instr, k=1):
        instr.then_inc(self.sem, k)
        self.n += k


class Stage:
    def __init__(self, name, engine, fn, deps=(), inc=1, pre=(), active=None,
                 ctr=None):
        self.name, self.engine, self.fn = name, engine, fn
        self.deps, self.inc_amt, self.pre = list(deps), inc, list(pre)
        self.active = active or (lambda i: True)
        self.ctr = ctr
        self.base = 0
        self.total = 0


_CURC = [None]


def run_pipeline(nc, stages, n, tag):
    by_eng, smap = {}, {}
    for s in stages:
        if s.ctr is None:
            c = _CURC[0]
            if c is not None:
                s.ctr = cpool(c, s.name)
            else:
                s.ctr = Ctr(nc, f"{tag}_{s.name}")
        s.base = s.ctr.n
        s.total = sum(1 for i in range(n) if s.active(i))
        smap[s.name] = s
        by_eng.setdefault(s.engine, []).append(s)

    def make_body(eng_stages):
        def body(eng):
            fired = {s.name: 0 for s in eng_stages}
            for i in range(n):
                for s in eng_stages:
                    if not s.active(i):
                        continue
                    if fired[s.name] == 0:
                        for (ctr, v) in s.pre:
                            eng.wait_ge(ctr.sem, v)
                    for dep in s.deps:
                        if isinstance(dep[0], str):
                            dst = smap[dep[0]]
                            k = dep[1](i)
                            if k <= 0:
                                continue
                            k = min(k, dst.total)
                            eng.wait_ge(dst.ctr.sem, dst.base + k * dst.inc_amt)
                        else:
                            ctr, fn_i, inc_amt, base = dep
                            k = fn_i(i)
                            if k <= 0:
                                continue
                            eng.wait_ge(ctr.sem, base + k * inc_amt)
                    if s.inc_amt == 16 and fired[s.name] > 0:
                        eng.wait_ge(s.ctr.sem, s.base + fired[s.name] * 16)
                    last = s.fn(eng, i)
                    assert last is not None, f"stage {s.name} None at {i}"
                    s.ctr.inc(last, s.inc_amt)
                    fired[s.name] += 1
        return body

    with nc.Block() as blk:
        for ename, eng_stages in by_eng.items():
            getattr(blk, ename)(make_body(eng_stages))
    return smap


def ext(stage_or_ctr, fn_i, inc_amt=None, base=None):
    if isinstance(stage_or_ctr, Stage):
        s = stage_or_ctr
        return (s.ctr, lambda i: min(fn_i(i), s.total), s.inc_amt, s.base)
    return (stage_or_ctr, fn_i, 1 if inc_amt is None else inc_amt,
            0 if base is None else base)


class Ctx:
    pass


def new_program(name, extra_inputs):
    nc = bass.Bass("TRN2", target_bir_lowering=False)
    c = Ctx()
    c.nc = nc
    c.name = name
    c.inputs = {}
    c._pool = {}
    c.aa_seq = 0
    c.h_pre = []
    for nm, shape, dt in extra_inputs:
        c.inputs[nm] = nc.dram_tensor(nm, list(shape), dt, kind="ExternalInput").ap()
    c.ones_col_f = nc.dram_tensor("c_ones_col_f", [1, 128], F32, kind="ExternalInput").ap()
    c.ident_in = nc.dram_tensor("c_ident", [128, 128], F32, kind="ExternalInput").ap()
    c.zeros_in = nc.dram_tensor("c_zeros", [128, 64], F32, kind="ExternalInput").ap()
    c.onescol_in = nc.dram_tensor("c_ones128", [128, 1], F32, kind="ExternalInput").ap()
    c.eps_in = nc.dram_tensor("c_eps", [128, 1], F32, kind="ExternalInput").ap()
    c.onesw_in = nc.dram_tensor("c_onesw", [128, 16], F32, kind="ExternalInput").ap()
    c.xin = nc.dram_tensor("xin", [DC, 128, T], F32, kind="ExternalInput").ap()
    c.xout = nc.dram_tensor("xout", [DC, 128, T], F32, kind="ExternalOutput").ap()

    c.pb = [nc.alloc_psum_tensor(f"bank{i}", [128, 512], F32) for i in range(7)]
    c.pbt = nc.alloc_psum_tensor("bank7t", [128, 1024], BF16)

    c.onesf = nc.alloc_sbuf_tensor("onesf", [1, 128], F32)
    c.onesb = nc.alloc_sbuf_tensor("onesb", [1, 128], BF16)
    c.ident = nc.alloc_sbuf_tensor("ident", [128, 128], BF16)
    c.zeros = nc.alloc_sbuf_tensor("zeros", [128, 64], BF16)
    c.zerosf = nc.alloc_sbuf_tensor("zerosf", [128, 64], F32)
    c.onescol = nc.alloc_sbuf_tensor("onescol", [128, 1], BF16)
    c.onecolf = nc.alloc_sbuf_tensor("onecolf", [128, 1], F32)
    c.eps = nc.alloc_sbuf_tensor("eps_sb", [128, 1], F32)
    c.ld = Ctr(nc, "ld_const")
    with nc.Block() as blk:
        @blk.gpsimd
        def _(g):
            c.ld.inc(g.dma_start(out=c.onesf[:], in_=c.ones_col_f), 16)
            c.ld.inc(g.dma_start(out=c.onesb[:], in_=c.ones_col_f), 16)
            c.ld.inc(g.dma_start(out=c.ident[:], in_=c.ident_in), 16)
            c.ld.inc(g.dma_start(out=c.zeros[:], in_=c.zeros_in), 16)
            c.ld.inc(g.dma_start(out=c.zerosf[:], in_=c.zeros_in), 16)
            c.ld.inc(g.dma_start(out=c.onescol[:], in_=c.onescol_in), 16)
            c.ld.inc(g.dma_start(out=c.onecolf[:], in_=c.onescol_in), 16)
            c.ld.inc(g.dma_start(out=c.eps[:], in_=c.eps_in), 16)
    c.consts_ready = (c.ld, c.ld.n)
    _CURC[0] = c
    rem = nc.sbuf_bytes_remaining - 256
    c.arena = nc.alloc_sbuf_tensor("arena", [128, rem], mybir.dt.uint8)
    c.arena_base = nc.lookup_mloc(c.arena).addr
    c.arena_size = rem
    c.aa_off = 0
    return c


def aa(c, name, shape, dtype):
    nbytes = int(np.prod(shape[1:])) * mybir.dt.size(dtype)
    off = (c.aa_off + 63) & ~63
    assert off + nbytes <= c.arena_size, \
        f"arena overflow at {name}: {off}+{nbytes} > {c.arena_size}"
    c.aa_off = off + nbytes
    c.aa_seq += 1
    return c.nc.alloc_sbuf_tensor_at(f"{name}_{c.aa_seq}", list(shape), dtype,
                                     offset=c.arena_base + off)


def aa_reset(c):
    c.aa_off = 0
    if hasattr(c, "_lns"):
        del c._lns
    if hasattr(c, "_lns_donor"):
        del c._lns_donor


_PFX_TOKENS = {"f0", "lc", "ml", "gl", "mo", "rp", "at", "mla", "rope"}


_SEM_ALIAS = {
    "ast": "store", "gst": "store", "st": "store", "xld": "load",
    "wold": "wld", "wqld": "wld", "wld13": "wld", "w2l": "wl",
    "latmm": "pmm", "latev": "pev", "qkmm": "pmm", "qkev": "pev",
    "qmm": "pmm", "qev": "pev", "rmm": "pmm",
    "bmm": "vmm", "badd": "vev", "amm": "rkmm", "asilu": "rkevac",
    "amul": "ksq", "rtr": "bc", "rcp": "apply",
    "n1": "rowv", "n2": "rows", "n3": "sq", "n4": "rkevac", "n5": "ksq",
    "n6": "bc", "n7": "apply", "gbm": "mm2", "gbc": "prevcol",
}


def cpool(c, name):
    parts = name.split("_")
    while len(parts) > 1 and parts[0] in _PFX_TOKENS:
        parts = parts[1:]
    key = "_".join(parts)
    key = _SEM_ALIAS.get(key, key)
    if key not in c._pool:
        c._pool[key] = Ctr(c.nc, f"pp_{key}")
    return c._pool[key]


def xin_tile(c, tt):
    return c.cur_xin.rearrange("c p t -> p c t")[:, :, tt * TT:(tt + 1) * TT]


def xout_tile(c, tt):
    return c.cur_xout.rearrange("c p t -> p c t")[:, :, tt * TT:(tt + 1) * TT]


def host_consts():
    return {
        "c_ones_col_f": np.ones((1, 128), dtype=np.float32),
        "c_ident": np.eye(128, dtype=np.float32),
        "c_zeros": np.zeros((128, 64), dtype=np.float32),
        "c_ones128": np.ones((128, 1), dtype=np.float32),
        "c_eps": np.full((128, 1), 1e-5, dtype=np.float32),
        "c_onesw": np.ones((128, 16), dtype=np.float32),
    }


def shard_x(xfull):
    out = []
    for cid in range(NCORES):
        xc = xfull[NSEQ * cid: NSEQ * (cid + 1)]
        fm = np.ascontiguousarray(xc.transpose(2, 0, 1))
        out.append(fm.reshape(DC, 128, T))
    return out


def unshard_x(shards):
    xs = []
    for cid in range(NCORES):
        fm = np.asarray(shards[cid]).reshape(D, NSEQ, SEQ)
        xs.append(np.ascontiguousarray(fm.transpose(1, 2, 0)))
    return np.concatenate(xs, axis=0)


L0FFN_EXTRA = [("w_r", (D, D), F32), ("w_k", (D, D), F32), ("w_v", (D, D), F32),
               ("mu", (128, 2 * DC), F32)]


def build_l0ffn_program():
    c = new_program("l0ffn", L0FFN_EXTRA)
    c.cur_xin, c.cur_xout = c.xin, c.xout
    aa_reset(c)
    emit_l0ffn(c, lambda n: c.inputs[n], "f0")
    return c


def emit_l0ffn(c, gi, pfx):
    nc = c.nc

    wr = aa(c, "wr", [128, DC * D], BF16)
    wk = aa(c, "wk", [128, DC * D], BF16)
    wv = aa(c, "wv", [128, DC * D], BF16)
    mu = aa(c, "mu_sb", [128, 2 * DC], F32)
    wl = cpool(c, pfx + "wl")
    hpre = list(c.h_pre)
    with nc.Block() as blk:
        @blk.gpsimd
        def _(g):
            for ctr, v in hpre:
                g.wait_ge(ctr.sem, v)
            for w_sb, nm in ((wr, "w_r"), (wk, "w_k"), (wv, "w_v")):
                prev = wl.n
                if prev > 0:
                    g.wait_ge(wl.sem, prev)
                i = g.dma_start(out=w_sb[:].rearrange("p (kc m) -> p kc m", kc=DC),
                                in_=gi(nm).rearrange("(kc p) m -> p kc m", p=128))
                wl.inc(i, 16)
            g.wait_ge(wl.sem, wl.n)
            wl.inc(g.dma_start(out=mu[:], in_=gi("mu")), 16)
    WL = wl.n

    xt = aa(c, "xt", [128, 2 * DC * TT], F32)
    PADW = TT + 1
    xnt = aa(c, "xnt", [128, 3 * DC * PADW], BF16)
    xb = aa(c, "xb", [128, DC * TT], BF16)
    xsq = aa(c, "xsq", [128, DC * TT], BF16)
    rows = aa(c, "rows", [1, 5 * TT], F32)
    tmp8 = aa(c, "tmpf8", [128, DC * TT], F32)
    dbuf = aa(c, "dbuf", [128, DC * TT], BF16)
    xr = aa(c, "xr", [128, DC * TT], BF16)
    xk = aa(c, "xk", [128, DC * TT], BF16)
    rsb = aa(c, "rsb", [128, 2 * DC * TT], BF16)
    kact = aa(c, "kact", [128, 2 * DC * TT], BF16)
    vtmp = aa(c, "vtmp", [128, 2 * TT], F32)
    R_M, R_V, R_A, R_B, R_T = (k * TT for k in range(5))
    ps_sum, ps_sq = c.pb[0][0:1, :], c.pb[1][0:1, :]
    bA, bB = c.pb[2], c.pb[3]
    psr, psk, psv = c.pb[4], c.pb[5], c.pb[6]

    def xts(tl, dc):
        return xt[:, ((tl % 2) * DC + dc) * TT:((tl % 2) * DC + dc + 1) * TT]

    def xnb(tl, dc, a, b):
        base = ((tl % 3) * DC + dc) * PADW
        return xnt[:, base + a: base + b]

    N = (NTT + 1) * DC
    tt_of = lambda i: i // DC
    w_of = lambda i: i - DC
    at_tile = lambda i: i % DC == 0
    ln_act = lambda i: i % DC == 0 and i < NTT * DC
    rk_act = lambda i: i < NTT * DC
    vw_act = lambda i: i >= DC

    def f_load(sy, i):
        tl = tt_of(i)
        return sy.dma_start(
            out=xt[:, (tl % 2) * DC * TT:((tl % 2) + 1) * DC * TT]
                .rearrange("p (c t) -> p c t", c=DC),
            in_=xin_tile(c, tl))

    def f_cast(g, i):
        tl = tt_of(i)
        last = None
        for dc in range(DC):
            last = g.tensor_copy(out=xb[:, dc * TT:(dc + 1) * TT], in_=xts(tl, dc))
        return last

    def f_sq(v, i):
        last = None
        for dc in range(DC):
            s = slice(dc * TT, (dc + 1) * TT)
            last = v.tensor_mul(out=xsq[:, s], in0=xb[:, s], in1=xb[:, s])
        return last

    def f_lnmm(te, i):
        for dc in range(DC):
            s = slice(dc * TT, (dc + 1) * TT)
            te.matmul(ps_sum, c.onescol[:, 0:1], xb[:, s],
                      start=(dc == 0), stop=(dc == DC - 1))
        last = None
        for dc in range(DC):
            s = slice(dc * TT, (dc + 1) * TT)
            last = te.matmul(ps_sq, c.onescol[:, 0:1], xsq[:, s],
                             start=(dc == 0), stop=(dc == DC - 1))
        return last

    def f_rowv(v, i):
        v.tensor_scalar(out=rows[:, R_M:R_M + TT], in0=ps_sum, scalar1=1.0 / D,
                        scalar2=None, op0=ALU.mult)
        v.tensor_scalar(out=rows[:, R_T:R_T + TT], in0=ps_sq, scalar1=1.0 / D,
                        scalar2=None, op0=ALU.mult)
        v.drain()
        v.tensor_mul(out=rows[:, R_V:R_V + TT], in0=rows[:, R_M:R_M + TT],
                     in1=rows[:, R_M:R_M + TT])
        v.drain()
        return v.tensor_sub(out=rows[:, R_V:R_V + TT], in0=rows[:, R_T:R_T + TT],
                            in1=rows[:, R_V:R_V + TT])

    def f_rows(sc, i):
        sc.activation(out=rows[:, R_T:R_T + TT], in_=rows[:, R_V:R_V + TT],
                      func=AF.Ln, bias=c.eps[0:1, 0:1], scale=1.0)
        sc.drain()
        return sc.activation(out=rows[:, R_A:R_A + TT], in_=rows[:, R_T:R_T + TT],
                             func=AF.Exp, scale=-0.5)

    def f_rowb(v, i):
        return v.scalar_tensor_tensor(out=rows[:, R_B:R_B + TT],
                                      in0=rows[:, R_M:R_M + TT], scalar=-1.0,
                                      in1=rows[:, R_A:R_A + TT],
                                      op0=ALU.mult, op1=ALU.mult)

    def f_bc(te, i):
        te.matmul(bA[:], c.onesf[:, 0:128], rows[:, R_A:R_A + TT], start=True, stop=True)
        return te.matmul(bB[:], c.onesf[:, 0:128], rows[:, R_B:R_B + TT],
                         start=True, stop=True)

    def f_apply(v, i):
        tl = tt_of(i)
        for dc in range(DC):
            v.tensor_mul(out=tmp8[:, dc * TT:(dc + 1) * TT], in0=xts(tl, dc), in1=bA[:])
        v.drain()
        last = None
        for dc in range(DC):
            last = v.tensor_add(out=xnb(tl, dc, 1, 1 + TT),
                                in0=tmp8[:, dc * TT:(dc + 1) * TT], in1=bB[:])
        return last

    def f_prevcol(v, i):
        tl = tt_of(i)
        last = None
        for dc in range(DC):
            if tl % TPS == 0:
                last = v.tensor_copy(out=xnb(tl, dc, 0, 1), in_=c.zeros[:, 0:1])
            else:
                last = v.tensor_copy(out=xnb(tl, dc, 0, 1),
                                     in_=xnb(tl - 1, dc, TT, TT + 1))
        return last

    def f_mixd(v, i):
        tl = tt_of(i)
        last = None
        for dc in range(DC):
            last = v.tensor_sub(out=dbuf[:, dc * TT:(dc + 1) * TT],
                                in0=xnb(tl, dc, 1, 1 + TT), in1=xnb(tl, dc, 0, TT))
        return last

    def f_mixs(sc, i):
        last = None
        for j, dst in ((0, xr), (1, xk)):
            for dc in range(DC):
                last = sc.activation(out=dst[:, dc * TT:(dc + 1) * TT],
                                     in_=dbuf[:, dc * TT:(dc + 1) * TT],
                                     func=AF.Copy, scale=mu[:, j * DC + dc:j * DC + dc + 1])
        return last

    def f_mixv(v, i):
        tl = tt_of(i)
        last = None
        for j, dst in ((0, xr), (1, xk)):
            for dc in range(DC):
                s = slice(dc * TT, (dc + 1) * TT)
                last = v.tensor_add(out=dst[:, s], in0=dst[:, s],
                                    in1=xnb(tl, dc, 0, TT))
        return last

    def f_rkmm(te, i):
        tl, dout = tt_of(i), i % DC
        for kc in range(DC):
            te.matmul(psr[:], wr[:, kc * D + dout * 128: kc * D + (dout + 1) * 128],
                      xr[:, kc * TT:(kc + 1) * TT], start=(kc == 0), stop=(kc == DC - 1))
        last = None
        for kc in range(DC):
            last = te.matmul(psk[:], wk[:, kc * D + dout * 128: kc * D + (dout + 1) * 128],
                             xk[:, kc * TT:(kc + 1) * TT],
                             start=(kc == 0), stop=(kc == DC - 1))
        return last

    def f_rkevac(sc, i):
        tl, dout = tt_of(i), i % DC
        b = (tl % 2) * DC
        sc.activation(out=rsb[:, (b + dout) * TT:(b + dout + 1) * TT], in_=psr[:],
                      func=AF.Sigmoid)
        return sc.activation(out=kact[:, (b + dout) * TT:(b + dout + 1) * TT],
                             in_=psk[:], func=AF.Relu)

    def f_ksq(v, i):
        tl, dout = tt_of(i), i % DC
        b = (tl % 2) * DC
        s = slice((b + dout) * TT, (b + dout + 1) * TT)
        return v.tensor_mul(out=kact[:, s], in0=kact[:, s], in1=kact[:, s])

    def f_vmm(te, i):
        w = w_of(i)
        tl, dout = w // DC, w % DC
        b = (tl % 2) * DC
        last = None
        for kc in range(DC):
            last = te.matmul(psv[:], wv[:, kc * D + dout * 128: kc * D + (dout + 1) * 128],
                             kact[:, (b + kc) * TT:(b + kc + 1) * TT],
                             start=(kc == 0), stop=(kc == DC - 1))
        return last

    def f_residm(v, i):
        w = w_of(i)
        dout = w % DC
        b = ((w // DC) % 2) * DC
        return v.tensor_mul(out=vtmp[:, (w % 2) * TT:(w % 2 + 1) * TT], in0=psv[:],
                            in1=rsb[:, (b + dout) * TT:(b + dout + 1) * TT])

    def f_resida(g, i):
        w = w_of(i)
        tl, dout = w // DC, w % DC
        dst = xts(tl, dout)
        return g.tensor_add(out=dst, in0=dst,
                            in1=vtmp[:, (w % 2) * TT:(w % 2 + 1) * TT])

    def f_store(sy, i):
        tl = w_of(i) // DC
        return sy.dma_start(
            out=xout_tile(c, tl),
            in_=xt[:, (tl % 2) * DC * TT:((tl % 2) + 1) * DC * TT]
                .rearrange("p (c t) -> p c t", c=DC))

    one_t = lambda i: tt_of(i) + 1
    stages = [
        Stage("load", "sync", f_load, inc=16, active=ln_act,
              deps=[("store", lambda i: tt_of(i) - 1)]),
        Stage("cast", "gpsimd", f_cast, active=ln_act, pre=[c.consts_ready],
              deps=[("load", one_t), ("lnmm", lambda i: tt_of(i)),
                    ("sq", lambda i: tt_of(i))]),
        Stage("sq", "vector", f_sq, active=ln_act, deps=[("cast", one_t)]),
        Stage("lnmm", "tensor", f_lnmm, active=ln_act, pre=[c.consts_ready, (wl, WL)],
              deps=[("cast", one_t), ("sq", one_t), ("rowv", lambda i: tt_of(i))]),
        Stage("rowv", "vector", f_rowv, active=ln_act, deps=[("lnmm", one_t)]),
        Stage("rows", "scalar", f_rows, active=ln_act, deps=[("rowv", one_t)]),
        Stage("rowb", "vector", f_rowb, active=ln_act,
              deps=[("rows", one_t), ("rowv", one_t)]),
        Stage("bc", "tensor", f_bc, active=ln_act,
              deps=[("rows", one_t), ("rowb", one_t), ("apply", lambda i: tt_of(i))]),
        Stage("apply", "vector", f_apply, active=ln_act,
              deps=[("bc", one_t), ("mixv", lambda i: tt_of(i) - 2),
                    ("prevcol", lambda i: tt_of(i) - 1)]),
        Stage("prevcol", "vector", f_prevcol, active=ln_act, pre=[c.consts_ready],
              deps=[("apply", lambda i: tt_of(i))]),
        Stage("mixd", "vector", f_mixd, active=ln_act,
              deps=[("apply", one_t), ("prevcol", one_t), ("mixs", lambda i: tt_of(i))]),
        Stage("mixs", "scalar", f_mixs, active=ln_act, pre=[(wl, WL)],
              deps=[("mixd", one_t), ("rkmm", lambda i: tt_of(i) * DC)]),
        Stage("mixv", "vector", f_mixv, active=ln_act,
              deps=[("mixs", one_t), ("prevcol", one_t)]),
        Stage("rkmm", "tensor", f_rkmm, active=rk_act, pre=[(wl, WL)],
              deps=[("mixv", lambda i: tt_of(i) + 1), ("rkevac", lambda i: i)]),
        Stage("rkevac", "scalar", f_rkevac, active=rk_act,
              deps=[("rkmm", lambda i: i + 1), ("vmm", lambda i: (tt_of(i) - 1) * DC),
                    ("residm", lambda i: (tt_of(i) - 1) * DC)]),
        Stage("ksq", "vector", f_ksq, active=rk_act, deps=[("rkevac", lambda i: i + 1)]),
        Stage("vmm", "tensor", f_vmm, active=vw_act,
              deps=[("ksq", lambda i: (w_of(i) // DC + 1) * DC),
                    ("residm", lambda i: w_of(i))]),
        Stage("residm", "vector", f_residm, active=vw_act,
              deps=[("vmm", lambda i: w_of(i) + 1), ("rkevac", lambda i: w_of(i) + 1),
                    ("resida", lambda i: w_of(i) - 1)]),
        Stage("resida", "gpsimd", f_resida, active=vw_act,
              deps=[("residm", lambda i: w_of(i) + 1), ("cast", lambda i: w_of(i) // DC + 1)]),
        Stage("store", "sync", f_store, inc=16,
              active=lambda i: i >= DC and (i - DC) % DC == DC - 1,
              deps=[("resida", lambda i: w_of(i) + 1)]),
    ]
    for s in stages:
        s.ctr = cpool(c, pfx + s.name)
        if s.name == "load":
            s.pre = list(s.pre) + hpre
    smap = run_pipeline(nc, stages, N, pfx)
    st = smap["store"]
    with nc.Block() as blk:
        @blk.sync
        def _(sy):
            sy.wait_ge(st.ctr.sem, st.base + st.total * 16)
    c.h_pre = [(st.ctr, st.base + st.total * 16)]
    return c


def prep_l0ffn(inputs):
    wr = np.ascontiguousarray(np.asarray(inputs["L0_ffn_W_r"], np.float32).T)
    wk = np.ascontiguousarray(np.asarray(inputs["L0_ffn_W_k"], np.float32).T)
    wv = np.ascontiguousarray(np.asarray(inputs["L0_ffn_W_v"], np.float32).T)
    mu = np.stack([np.asarray(inputs["L0_ffn_mu_r"], np.float32).reshape(DC, 128),
                   np.asarray(inputs["L0_ffn_mu_k"], np.float32).reshape(DC, 128)], 0)
    mu = np.ascontiguousarray(mu.transpose(2, 0, 1).reshape(128, 2 * DC))
    return {"w_r": wr, "w_k": wk, "w_v": wv, "mu": mu}




def ln_scratch(c):
    nc = c.nc
    if not hasattr(c, "_lns"):
        donor = getattr(c, "_lns_donor", None)
        if donor is not None:
            base = nc.lookup_mloc(donor).addr
            c._lns = (nc.alloc_sbuf_tensor_at("lns_xb", [128, 2 * TT], BF16,
                                              offset=base),
                      nc.alloc_sbuf_tensor_at("lns_xsq", [128, 2 * TT], BF16,
                                              offset=base + 2 * TT * 2),
                      nc.alloc_sbuf_tensor_at("lns_rows", [1, 5 * TT], F32,
                                              offset=base + 4 * TT * 2),
                      nc.alloc_sbuf_tensor_at("lns_tmp8", [128, 2 * TT], F32,
                                              offset=base + 4 * TT * 2 + 5 * TT * 4))
        else:
            c._lns = (aa(c, "lns_xb", [128, 2 * TT], BF16),
                      aa(c, "lns_xsq", [128, 2 * TT], BF16),
                      aa(c, "lns_rows", [1, 5 * TT], F32),
                      aa(c, "lns_tmp8", [128, 2 * TT], F32))
    return c._lns


def ln_pipeline(c, tag, tiles, xt, xn_dst, pre_load=()):
    nc = c.nc
    nt = len(tiles)
    xb, xsq, rows, tmp8 = ln_scratch(c)
    R_M, R_V, R_A, R_B, R_T = (k * TT for k in range(5))
    ps_sum, ps_sq = c.pb[0][0:1, :], c.pb[1][0:1, :]
    bA, bB = c.pb[2], c.pb[3]

    def xts(k, dc):
        return xt[:, ((k % 2) * DC + dc) * TT:((k % 2) * DC + dc + 1) * TT]

    tl_of = lambda i: i // DC
    dc_of = lambda i: i % DC
    w_of = lambda i: i - DC

    def f_load(sy, i):
        k = tl_of(i)
        return sy.dma_start(
            out=xt[:, (k % 2) * DC * TT:((k % 2) + 1) * DC * TT]
                .rearrange("p (c t) -> p c t", c=DC),
            in_=xin_tile(c, tiles[k]))

    def f_cast(g, i):
        k, dc = tl_of(i), dc_of(i)
        return g.tensor_copy(out=xb[:, (i % 2) * TT:(i % 2 + 1) * TT], in_=xts(k, dc))

    def f_sq(v, i):
        s = slice((i % 2) * TT, (i % 2 + 1) * TT)
        return v.tensor_mul(out=xsq[:, s], in0=xb[:, s], in1=xb[:, s])

    def f_mm2(te, i):
        dc = dc_of(i)
        s = slice((i % 2) * TT, (i % 2 + 1) * TT)
        te.matmul(ps_sum, c.onescol[:, 0:1], xb[:, s],
                  start=(dc == 0), stop=(dc == DC - 1))
        return te.matmul(ps_sq, c.onescol[:, 0:1], xsq[:, s],
                         start=(dc == 0), stop=(dc == DC - 1))

    def f_rowv(v, i):
        v.tensor_scalar(out=rows[:, R_M:R_M + TT], in0=ps_sum, scalar1=1.0 / D,
                        scalar2=None, op0=ALU.mult)
        v.tensor_scalar(out=rows[:, R_T:R_T + TT], in0=ps_sq, scalar1=1.0 / D,
                        scalar2=None, op0=ALU.mult)
        v.drain()
        v.tensor_mul(out=rows[:, R_V:R_V + TT], in0=rows[:, R_M:R_M + TT],
                     in1=rows[:, R_M:R_M + TT])
        v.drain()
        return v.tensor_sub(out=rows[:, R_V:R_V + TT], in0=rows[:, R_T:R_T + TT],
                            in1=rows[:, R_V:R_V + TT])

    def f_rows(sc, i):
        sc.activation(out=rows[:, R_T:R_T + TT], in_=rows[:, R_V:R_V + TT],
                      func=AF.Ln, bias=c.eps[0:1, 0:1], scale=1.0)
        sc.drain()
        return sc.activation(out=rows[:, R_A:R_A + TT], in_=rows[:, R_T:R_T + TT],
                             func=AF.Exp, scale=-0.5)

    def f_rowb(v, i):
        return v.scalar_tensor_tensor(out=rows[:, R_B:R_B + TT],
                                      in0=rows[:, R_M:R_M + TT], scalar=-1.0,
                                      in1=rows[:, R_A:R_A + TT],
                                      op0=ALU.mult, op1=ALU.mult)

    def f_bc(te, i):
        te.matmul(bA[:], c.onesf[:, 0:128], rows[:, R_A:R_A + TT], start=True, stop=True)
        return te.matmul(bB[:], c.onesf[:, 0:128], rows[:, R_B:R_B + TT],
                         start=True, stop=True)

    def f_apply(v, i):
        w = w_of(i)
        k, dc = w // DC, w % DC
        sl = slice((i % 2) * TT, (i % 2 + 1) * TT)
        v.tensor_mul(out=tmp8[:, sl], in0=xts(k, dc), in1=bA[:])
        v.drain()
        return v.tensor_add(out=xn_dst(k, dc), in0=tmp8[:, sl], in1=bB[:])

    N = (nt + 1) * DC
    ld_act = lambda i: i % DC == 0 and i < nt * DC
    st_act = lambda i: i < nt * DC
    ap_act = lambda i: i >= DC
    stages = [
        Stage("load", "sync", f_load, inc=16, active=ld_act,
              deps=[("apply", lambda i: (tl_of(i) - 1) * DC)], pre=list(pre_load)),
        Stage("cast", "gpsimd", f_cast, active=st_act, pre=[c.consts_ready],
              deps=[("load", lambda i: tl_of(i) + 1), ("mm2", lambda i: i - 1),
                    ("sq", lambda i: i - 1)]),
        Stage("sq", "vector", f_sq, active=st_act, deps=[("cast", lambda i: i + 1)]),
        Stage("mm2", "tensor", f_mm2, active=st_act, pre=[c.consts_ready],
              deps=[("cast", lambda i: i + 1), ("sq", lambda i: i + 1),
                    ("rowv", lambda i: tl_of(i))]),
        Stage("rowv", "vector", f_rowv, active=lambda i: st_act(i) and dc_of(i) == DC - 1,
              deps=[("mm2", lambda i: i + 1)]),
        Stage("rows", "scalar", f_rows, active=lambda i: st_act(i) and dc_of(i) == DC - 1,
              deps=[("rowv", lambda i: tl_of(i) + 1)]),
        Stage("rowb", "vector", f_rowb, active=lambda i: st_act(i) and dc_of(i) == DC - 1,
              deps=[("rows", lambda i: tl_of(i) + 1), ("rowv", lambda i: tl_of(i) + 1)]),
        Stage("bc", "tensor", f_bc, active=lambda i: st_act(i) and dc_of(i) == DC - 1,
              deps=[("rows", lambda i: tl_of(i) + 1), ("rowb", lambda i: tl_of(i) + 1),
                    ("apply", lambda i: tl_of(i) * DC)]),
        Stage("apply", "vector", f_apply, active=ap_act,
              deps=[("bc", lambda i: w_of(i) // DC + 1)]),
    ]
    smap = run_pipeline(nc, stages, N, tag)
    return smap


H = 16
HD = 64
KC = SEQ // 128
VW = 65


ATTN_EXTRA = {
    "mla": [("w_kv_down", (D, 128), F32), ("w_kv_up", (128, 2 * D), F32),
            ("w_q", (D, D), F32), ("w_out", (D, D), F32),
            ("c_tril", (128, 128), F32)],
    "rope": [("qkv_wT", (D, 3 * D), F32), ("qkv_b", (128, 24), F32),
             ("out_wT", (D, D), F32), ("out_b", (128, DC), F32),
             ("vb_row", (1, D), F32),
             ("rope_ca", (128, SEQ), F32), ("rope_sb", (128, SEQ), F32)],
}


def build_attn_program(kind):
    c = new_program(f"attn_{kind}", ATTN_EXTRA[kind])
    c.cur_xin, c.cur_xout = c.xin, c.xout
    aa_reset(c)
    emit_attn(c, lambda n: c.inputs[n], kind, "at_" + kind + "_")
    return c


def emit_attn(c, gi, kind, pfx):
    nc = c.nc

    xt = aa(c, "a_xt", [128, 2 * DC * TT], F32)
    xn = aa(c, "a_xn", [128, DC * SEQ], BF16)
    qf = aa(c, "a_q", [128, DC * SEQ], BF16)
    kf = aa(c, "a_k", [128, DC * SEQ], BF16)
    c._lns_donor = qf
    vt = aa(c, "a_v", [128, KC * H * VW], BF16)
    strip = xn
    afm = xn[:, 8192:8192 + DC * TT]
    atm = xn[:, 8192 + DC * TT: 8192 + DC * TT + 4 * D]
    ytile = aa(c, "a_y", [128, 2 * TT], BF16)
    rdiv = aa(c, "a_rdiv", [128, 4], F32)
    onesw = aa(c, "a_onesw", [128, H], BF16)
    woring = aa(c, "a_woring", [128, 2 * DC * 128], BF16)

    if kind == "mla":
        lat = aa(c, "a_lat", [128, SEQ], BF16)
        wkvd = aa(c, "a_wkvd", [128, DC * 128], BF16)
        wkvu = aa(c, "a_wkvu", [128, 2 * D], BF16)
        wqring = aa(c, "a_wqring", [128, 2 * DC * 128], BF16)
        tril = aa(c, "a_tril", [128, 128], BF16)
    else:
        wring = aa(c, "a_wring", [128, 2 * DC * 128], BF16)
        wv8 = aa(c, "a_wv8", [128, DC * D], BF16)
        qkb = aa(c, "a_qkb", [128, 24], F32)
        vbb = aa(c, "a_vbb", [128, D], BF16)
        outb = aa(c, "a_outb", [128, DC], F32)
        rca = aa(c, "a_rca", [128, SEQ], BF16)
        rsb2 = aa(c, "a_rsb2", [128, SEQ], BF16)
        rtmp = xn[:, 0:6 * TT]

    wl = cpool(c, pfx + "wl")
    hpre = list(c.h_pre)
    with nc.Block() as blk:
        @blk.gpsimd
        def _(g):
            for ctr, v in hpre:
                g.wait_ge(ctr.sem, v)
            if kind == "mla":
                wl.inc(g.dma_start(out=wkvd[:].rearrange("p (kc m) -> p kc m", kc=DC),
                                   in_=gi("w_kv_down").rearrange("(kc p) m -> p kc m", p=128)), 16)
                wl.inc(g.dma_start(out=wkvu[:], in_=gi("w_kv_up")), 16)
                wl.inc(g.dma_start(out=tril[:], in_=gi("c_tril")), 16)
            else:
                wl.inc(g.dma_start(out=wv8[:].rearrange("p (kc m) -> p kc m", kc=DC),
                                   in_=gi("qkv_wT").rearrange("(kc p) m -> p kc m", p=128)[:, :, 2 * D:3 * D]), 16)
                wl.inc(g.dma_start(out=qkb[:], in_=gi("qkv_b")), 16)
                wl.inc(g.dma_start(out=outb[:], in_=gi("out_b")), 16)
                wl.inc(g.dma_start(out=rca[:], in_=gi("rope_ca")), 16)
                wl.inc(g.dma_start(out=rsb2[:], in_=gi("rope_sb")), 16)
            wl.inc(g.dma_start(out=onesw[:], in_=c.onesw_in), 16)
    WL = wl.n

    if kind == "rope":
        vbrow = aa(c, "a_vbrow", [1, D], F32)
        vbd = cpool(c, pfx + "vbd")
        vb = cpool(c, pfx + "vb")
        with nc.Block() as blk:
            @blk.gpsimd
            def _(g):
                vbd.inc(g.dma_start(out=vbrow[:], in_=gi("vb_row")), 16)

            @blk.tensor
            def _(te):
                te.wait_ge(vbd.sem, 16)
                te.wait_ge(c.ld.sem, c.consts_ready[1])
                te.matmul(c.pb[0][:], c.onesf[:, 0:128], vbrow[:, 0:512],
                          start=True, stop=True)
                vb.inc(te.matmul(c.pb[1][:], c.onesf[:, 0:128], vbrow[:, 512:1024],
                                 start=True, stop=True), 1)

            @blk.vector
            def _(v):
                v.wait_ge(vb.sem, 1)
                v.tensor_copy(out=vbb[:, 0:512], in_=c.pb[0][:])
                vb.inc(v.tensor_copy(out=vbb[:, 512:1024], in_=c.pb[1][:]), 1)
        VB = vb.n

    store_last = None
    for s in range(NSEQ):
        seq_tiles = [s * TPS + k for k in range(TPS)]

        def xn_dst(k, dc):
            return xn[:, (dc * TPS + k) * TT:(dc * TPS + k + 1) * TT]

        def xns(dc, col, w):
            return xn[:, dc * SEQ + col: dc * SEQ + col + w]

        pre_load = list(hpre)
        if store_last is not None:
            pre_load = [(store_last.ctr, store_last.base + store_last.total * 16)]
        ln_pipeline(c, f"a{s}ln", seq_tiles, xt, xn_dst, pre_load=pre_load)

        if kind == "mla":
            def f_latmm(te, i):
                last = None
                for kc in range(DC):
                    last = te.matmul(c.pb[4][:], wkvd[:, kc * 128:(kc + 1) * 128],
                                     xns(kc, i * TT, TT),
                                     start=(kc == 0), stop=(kc == DC - 1))
                return last

            def f_latev(sc, i):
                return sc.activation(out=lat[:, i * TT:(i + 1) * TT], in_=c.pb[4][:],
                                     func=AF.Copy)

            run_pipeline(nc, [
                Stage("latmm", "tensor", f_latmm, pre=[(wl, WL)],
                      deps=[("latev", lambda i: i)]),
                Stage("latev", "scalar", f_latev, deps=[("latmm", lambda i: i + 1)]),
            ], 4, f"a{s}lat")

            def f_wqld(g, i):
                oc = i // 4
                return g.dma_start(
                    out=wqring[:, (oc % 2) * DC * 128:((oc % 2) + 1) * DC * 128]
                        .rearrange("p (kc m) -> p kc m", kc=DC),
                    in_=gi("w_q").rearrange("(kc p) m -> p kc m", p=128)
                        [:, :, oc * 128:(oc + 1) * 128])

            def f_qmm(te, i):
                oc, qt = divmod(i, 4)
                last = None
                for kc in range(DC):
                    last = te.matmul(
                        c.pb[4 + i % 2][:],
                        wqring[:, ((oc % 2) * DC + kc) * 128:((oc % 2) * DC + kc + 1) * 128],
                        xns(kc, qt * TT, TT),
                        start=(kc == 0), stop=(kc == DC - 1))
                return last

            def f_qev(sc, i):
                oc, qt = divmod(i, 4)
                return sc.activation(out=qf[:, oc * SEQ + qt * TT: oc * SEQ + (qt + 1) * TT],
                                     in_=c.pb[4 + i % 2][:], func=AF.Copy)

            def f_kmm(te, i):
                oc, qt = divmod(i, 4)
                return te.matmul(c.pb[6][:], wkvu[:, oc * 128:(oc + 1) * 128],
                                 lat[:, qt * TT:(qt + 1) * TT], start=True, stop=True)

            def f_kev(sc, i):
                oc, qt = divmod(i, 4)
                return sc.activation(out=kf[:, oc * SEQ + qt * TT: oc * SEQ + (qt + 1) * TT],
                                     in_=c.pb[6][:], func=AF.Copy)

            run_pipeline(nc, [
                Stage("wqld", "gpsimd", f_wqld, inc=16, active=lambda i: i % 4 == 0,
                      deps=[("qmm", lambda i: (i // 4 - 1) * 4)]),
                Stage("qmm", "tensor", f_qmm,
                      deps=[("qev", lambda i: i - 1), ("wqld", lambda i: i // 4 + 1)]),
                Stage("qev", "scalar", f_qev, deps=[("qmm", lambda i: i + 1)]),
                Stage("kmm", "tensor", f_kmm, deps=[("kev", lambda i: i)]),
                Stage("kev", "scalar", f_kev, deps=[("kmm", lambda i: i + 1)]),
            ], 32, f"a{s}qk")

            def f_vmm(te, i):
                tc, half = divmod(i, 2)
                return te.matmul(c.pb[4 + i % 2][:], lat[:, tc * 128:(tc + 1) * 128],
                                 wkvu[:, D + half * 512: D + (half + 1) * 512],
                                 start=True, stop=True)

            def f_vev(v, i):
                tc, half = divmod(i, 2)
                st0 = tc * H * VW + half * 8 * VW
                dst = vt[:, st0: st0 + 8 * VW].rearrange(
                    "p (h w) -> p h w", w=VW)[:, :, 0:64]
                return v.tensor_copy(out=dst, in_=c.pb[4 + i % 2][:].rearrange(
                    "p (h e) -> p h e", e=64))

            def f_vones(v, i):
                tc = i // 2
                dst = vt[:, tc * H * VW: (tc + 1) * H * VW].rearrange(
                    "p (h w) -> p h w", w=VW)[:, :, 64:65]
                return v.tensor_copy(out=dst, in_=onesw[:].rearrange("p (h o) -> p h o", o=1))

            run_pipeline(nc, [
                Stage("vmm", "tensor", f_vmm, deps=[("vev", lambda i: i - 1)]),
                Stage("vev", "vector", f_vev, deps=[("vmm", lambda i: i + 1)]),
                Stage("vones", "vector", f_vones, active=lambda i: i % 2 == 1,
                      pre=[c.consts_ready, (wl, WL)]),
            ], 32, f"a{s}v")
        else:
            def f_wld(g, i):
                oc = i // 4
                return g.dma_start(
                    out=wring[:, (oc % 2) * DC * 128:((oc % 2) + 1) * DC * 128]
                        .rearrange("p (kc m) -> p kc m", kc=DC),
                    in_=gi("qkv_wT").rearrange("(kc p) m -> p kc m", p=128)
                        [:, :, oc * 128:(oc + 1) * 128])

            def f_qkmm(te, i):
                oc, qt = divmod(i, 4)
                last = None
                for kc in range(DC):
                    last = te.matmul(
                        c.pb[4 + i % 2][:],
                        wring[:, ((oc % 2) * DC + kc) * 128:((oc % 2) * DC + kc + 1) * 128],
                        xns(kc, qt * TT, TT), start=(kc == 0), stop=(kc == DC - 1))
                return last

            def f_qkev(sc, i):
                oc, qt = divmod(i, 4)
                dst = (qf if oc < 8 else kf)
                o = oc % 8
                return sc.activation(out=dst[:, o * SEQ + qt * TT: o * SEQ + (qt + 1) * TT],
                                     in_=c.pb[4 + i % 2][:], func=AF.Identity,
                                     bias=qkb[:, oc:oc + 1], scale=1.0)

            run_pipeline(nc, [
                Stage("wld", "gpsimd", f_wld, inc=16, active=lambda i: i % 4 == 0,
                      deps=[("qkmm", lambda i: (i // 4 - 1) * 4)]),
                Stage("qkmm", "tensor", f_qkmm,
                      deps=[("wld", lambda i: i // 4 + 1), ("qkev", lambda i: i - 1)],
                      pre=[(wl, WL)]),
                Stage("qkev", "scalar", f_qkev, deps=[("qkmm", lambda i: i + 1)]),
            ], 64, f"a{s}qk")

            def f_vmm(te, i):
                tc, half = divmod(i, 2)
                last = None
                for cc in range(DC):
                    last = te.matmul(c.pb[4 + i % 2][:], xns(cc, tc * 128, 128),
                                     wv8[:, cc * D + half * 512: cc * D + (half + 1) * 512],
                                     start=(cc == 0), stop=(cc == DC - 1))
                return last

            def f_vev(v, i):
                tc, half = divmod(i, 2)
                st0 = tc * H * VW + half * 8 * VW
                dst = vt[:, st0: st0 + 8 * VW].rearrange(
                    "p (h w) -> p h w", w=VW)[:, :, 0:64]
                return v.tensor_add(out=dst, in0=c.pb[4 + i % 2][:].rearrange(
                    "p (h e) -> p h e", e=64),
                    in1=vbb[:, half * 512:(half + 1) * 512].rearrange(
                        "p (h e) -> p h e", e=64))

            def f_vones(v, i):
                tc = i // 2
                dst = vt[:, tc * H * VW: (tc + 1) * H * VW].rearrange(
                    "p (h w) -> p h w", w=VW)[:, :, 64:65]
                return v.tensor_copy(out=dst, in_=onesw[:].rearrange("p (h o) -> p h o", o=1))

            run_pipeline(nc, [
                Stage("vmm", "tensor", f_vmm, deps=[("vev", lambda i: i - 1)],
                      pre=[(wl, WL)]),
                Stage("vev", "vector", f_vev, deps=[("vmm", lambda i: i + 1)],
                      pre=[(vb, VB)]),
                Stage("vones", "vector", f_vones, active=lambda i: i % 2 == 1,
                      pre=[c.consts_ready, (wl, WL)]),
            ], 32, f"a{s}v")

            def rtile(j, dc, qt):
                buf = qf if j == 0 else kf
                return buf[:, dc * SEQ + qt * TT: dc * SEQ + (qt + 1) * TT]

            def f_rsw(g, i):
                j, r = divmod(i, 32)
                dc, qt = divmod(r, 4)
                o = (i % 2) * 3 * TT
                t = rtile(j, dc, qt)
                last = None
                for grp in range(4):
                    srcg = grp ^ 1
                    last = g.tensor_copy(out=rtmp[grp * 32:(grp + 1) * 32, o:o + TT],
                                         in_=t[srcg * 32:(srcg + 1) * 32, :])
                return last

            def f_rvec(v, i):
                j, r = divmod(i, 32)
                dc, qt = divmod(r, 4)
                o = (i % 2) * 3 * TT
                t = rtile(j, dc, qt)
                ca = rca[:, qt * TT:(qt + 1) * TT]
                sb = rsb2[:, qt * TT:(qt + 1) * TT]
                v.tensor_mul(out=rtmp[:, o + TT:o + 2 * TT], in0=t, in1=ca)
                v.tensor_mul(out=rtmp[:, o + 2 * TT:o + 3 * TT],
                             in0=rtmp[:, o:o + TT], in1=sb)
                v.drain()
                return v.tensor_add(out=t, in0=rtmp[:, o + TT:o + 2 * TT],
                                    in1=rtmp[:, o + 2 * TT:o + 3 * TT])

            run_pipeline(nc, [
                Stage("rsw", "gpsimd", f_rsw, deps=[("rvec", lambda i: i - 1)],
                      pre=[(wl, WL)]),
                Stage("rvec", "vector", f_rvec, deps=[("rsw", lambda i: i + 1)]),
            ], 64, f"a{s}rope")

        SUB = 28
        NU = 4 * H

        def klim(qt):
            return 4 * qt + 4 if kind == "mla" else KC

        CUMU = [0]
        for u in range(NU):
            CUMU.append(CUMU[-1] + klim(u // H))
        def parts(i):
            u, sub = divmod(i, SUB)
            return u // H, u % H, sub, u

        def pvlim(qt, j):
            return 4 * qt + j + 1 if kind == "mla" else KC

        def f_xld(sy, i):
            qt = parts(i)[0]
            return sy.dma_start(
                out=xt[:, (qt % 2) * DC * TT:((qt % 2) + 1) * DC * TT]
                    .rearrange("p (c t) -> p c t", c=DC),
                in_=xin_tile(c, s * TPS + qt))

        def f_scmm(te, i):
            qt, h, kc, u = parts(i)
            r0 = 64 * (h % 2)
            hc = h // 2
            return te.matmul(c.pb[4 + kc % 2][:],
                             kf[r0:r0 + 64, hc * SEQ + kc * 128: hc * SEQ + (kc + 1) * 128],
                             qf[r0:r0 + 64, hc * SEQ + qt * TT: hc * SEQ + (qt + 1) * TT],
                             start=True, stop=True)

        def f_scev(sc, i):
            qt, h, kc, u = parts(i)
            return sc.activation(out=strip[:, kc * TT:(kc + 1) * TT],
                                 in_=c.pb[4 + kc % 2][:], func=AF.Exp, scale=0.125)

        def f_scmask(v, i):
            qt, h, kc, u = parts(i)
            j = kc - 4 * qt
            sl = slice(kc * TT + j * 128, kc * TT + (j + 1) * 128)
            return v.tensor_mul(out=strip[:, sl], in0=strip[:, sl], in1=tril[:])

        def f_pvmm(te, i):
            qt, h, sub, u = parts(i)
            j = sub - 16
            bank = c.pb[[6, 0][j % 2]]
            lim = pvlim(qt, j)
            last = None
            for kc in range(lim):
                last = te.matmul(bank[:, 0:VW],
                                 strip[:, kc * TT + j * 128: kc * TT + (j + 1) * 128],
                                 vt[:, (kc * H + h) * VW:(kc * H + h + 1) * VW],
                                 start=(kc == 0), stop=(kc == lim - 1))
            return last

        def f_pvrec(v, i):
            qt, h, sub, u = parts(i)
            j = sub - 16
            bank = c.pb[[6, 0][j % 2]]
            return v.reciprocal(out=rdiv[:, j % 2:j % 2 + 1], in_=bank[:, 64:65])

        def f_pvdiv(sc, i):
            qt, h, sub, u = parts(i)
            j = sub - 16
            bank = c.pb[[6, 0][j % 2]]
            return sc.activation(out=atm[:, j * D + h * 64: j * D + (h + 1) * 64],
                                 in_=bank[:, 0:64], func=AF.Copy,
                                 scale=rdiv[:, j % 2:j % 2 + 1])

        def f_eptr(te, i):
            qt, h, sub, u = parts(i)
            j = sub - 16
            last = None
            for dc in range(DC):
                last = te.transpose(c.pbt[:, dc * 128:(dc + 1) * 128],
                                    atm[:, j * D + dc * 128: j * D + (dc + 1) * 128],
                                    c.ident[:])
            return last

        def f_epcp(v, i):
            qt, h, sub, u = parts(i)
            j = sub - 16
            last = None
            for dc in range(DC):
                last = v.tensor_copy(out=afm[:, dc * TT + j * 128: dc * TT + (j + 1) * 128],
                                     in_=c.pbt[:, dc * 128:(dc + 1) * 128])
            return last

        wo_src = gi("w_out" if kind == "mla" else "out_wT")

        def f_wold(g, i):
            qt, h, sub, u = parts(i)
            dout = sub - 20
            return g.dma_start(
                out=woring[:, (dout % 2) * DC * 128:((dout % 2) + 1) * DC * 128]
                    .rearrange("p (kc m) -> p kc m", kc=DC),
                in_=wo_src.rearrange("(kc p) m -> p kc m", p=128)
                    [:, :, dout * 128:(dout + 1) * 128])

        def f_womm(te, i):
            qt, h, sub, u = parts(i)
            dout = sub - 20
            bank = c.pb[1 + dout % 2]
            last = None
            for kc in range(DC):
                last = te.matmul(
                    bank[:],
                    woring[:, ((dout % 2) * DC + kc) * 128:((dout % 2) * DC + kc + 1) * 128],
                    afm[:, kc * TT:(kc + 1) * TT],
                    start=(kc == 0), stop=(kc == DC - 1))
            return last

        def f_woev(sc, i):
            qt, h, sub, u = parts(i)
            dout = sub - 20
            bank = c.pb[1 + dout % 2]
            if kind == "rope":
                return sc.activation(out=ytile[:, (dout % 2) * TT:(dout % 2 + 1) * TT],
                                     in_=bank[:], func=AF.Identity,
                                     bias=outb[:, dout:dout + 1], scale=1.0)
            return sc.activation(out=ytile[:, (dout % 2) * TT:(dout % 2 + 1) * TT],
                                 in_=bank[:], func=AF.Copy)

        def f_woadd(g, i):
            qt, h, sub, u = parts(i)
            dout = sub - 20
            dst = xt[:, ((qt % 2) * DC + dout) * TT:((qt % 2) * DC + dout + 1) * TT]
            return g.tensor_add(out=dst, in0=dst,
                                in1=ytile[:, (dout % 2) * TT:(dout % 2 + 1) * TT])

        def f_ast(sy, i):
            qt = parts(i)[0]
            return sy.dma_start(
                out=xout_tile(c, s * TPS + qt),
                in_=xt[:, (qt % 2) * DC * TT:((qt % 2) + 1) * DC * TT]
                    .rearrange("p (c t) -> p c t", c=DC))

        is_mla = kind == "mla"
        stages = [
            Stage("xld", "sync", f_xld, inc=16,
                  active=lambda i: parts(i)[1] == 0 and parts(i)[2] == 0,
                  deps=[("ast", lambda i: parts(i)[0] - 1)]),
            Stage("scmm", "tensor", f_scmm,
                  active=lambda i: parts(i)[2] < klim(parts(i)[0]),
                  deps=[("scev", lambda i: CUMU[parts(i)[3]] + parts(i)[2] - 1)]),
            Stage("scev", "scalar", f_scev,
                  active=lambda i: parts(i)[2] < klim(parts(i)[0]),
                  deps=[("scmm", lambda i: CUMU[parts(i)[3]] + parts(i)[2] + 1),
                        ("pvmm", lambda i: 4 * parts(i)[3])]),
        ]
        if is_mla:
            stages.append(Stage(
                "scmask", "vector", f_scmask,
                active=lambda i: parts(i)[2] < klim(parts(i)[0])
                                 and 0 <= parts(i)[2] - 4 * parts(i)[0] < 4,
                deps=[("scev", lambda i: CUMU[parts(i)[3]] + parts(i)[2] + 1)],
                pre=[(wl, WL)]))
        def pv_deps(i):
            qt, h, sub, u = parts(i)
            j = sub - 16
            return CUMU[u] + pvlim(qt, j)
        stages += [
            Stage("pvmm", "tensor", f_pvmm,
                  active=lambda i: 16 <= parts(i)[2] < 20,
                  deps=([("scev", pv_deps)] +
                        ([("scmask", lambda i: 4 * parts(i)[3] + (parts(i)[2] - 16) + 1)]
                         if is_mla else []) +
                        [("pvdiv", lambda i: 4 * parts(i)[3] + (parts(i)[2] - 16) - 1)])),
            Stage("pvrec", "vector", f_pvrec,
                  active=lambda i: 16 <= parts(i)[2] < 20,
                  deps=[("pvmm", lambda i: 4 * parts(i)[3] + (parts(i)[2] - 16) + 1),
                        ("pvdiv", lambda i: 4 * parts(i)[3] + (parts(i)[2] - 16) - 1)]),
            Stage("pvdiv", "scalar", f_pvdiv,
                  active=lambda i: 16 <= parts(i)[2] < 20,
                  deps=[("pvrec", lambda i: 4 * parts(i)[3] + (parts(i)[2] - 16) + 1),
                        ("eptr", lambda i: (parts(i)[0] - 1) * 4 + (parts(i)[2] - 16) + 1)]),
            Stage("eptr", "tensor", f_eptr,
                  active=lambda i: parts(i)[1] == H - 1 and 16 <= parts(i)[2] < 20,
                  deps=[("pvdiv", lambda i: parts(i)[3] * 4 + (parts(i)[2] - 16) + 1),
                        ("epcp", lambda i: parts(i)[0] * 4 + (parts(i)[2] - 16))],
                  pre=[c.consts_ready]),
            Stage("epcp", "vector", f_epcp,
                  active=lambda i: parts(i)[1] == H - 1 and 16 <= parts(i)[2] < 20,
                  deps=[("eptr", lambda i: parts(i)[0] * 4 + (parts(i)[2] - 16) + 1),
                        ("womm", lambda i: (parts(i)[0] - 1) * 8 + 8)]),
            Stage("wold", "gpsimd", f_wold, inc=16,
                  active=lambda i: parts(i)[1] == H - 1 and 20 <= parts(i)[2],
                  deps=[("womm", lambda i: parts(i)[0] * 8 + (parts(i)[2] - 20) - 1)]),
            Stage("womm", "tensor", f_womm,
                  active=lambda i: parts(i)[1] == H - 1 and 20 <= parts(i)[2],
                  deps=[("epcp", lambda i: (parts(i)[0] + 1) * 4),
                        ("wold", lambda i: parts(i)[0] * 8 + (parts(i)[2] - 20) + 1),
                        ("woev", lambda i: parts(i)[0] * 8 + (parts(i)[2] - 20) - 1)],
                  pre=[(wl, WL)]),
            Stage("woev", "scalar", f_woev,
                  active=lambda i: parts(i)[1] == H - 1 and 20 <= parts(i)[2],
                  deps=[("womm", lambda i: parts(i)[0] * 8 + (parts(i)[2] - 20) + 1),
                        ("woadd", lambda i: parts(i)[0] * 8 + (parts(i)[2] - 20) - 1)]),
            Stage("woadd", "gpsimd", f_woadd,
                  active=lambda i: parts(i)[1] == H - 1 and 20 <= parts(i)[2],
                  deps=[("woev", lambda i: parts(i)[0] * 8 + (parts(i)[2] - 20) + 1),
                        ("xld", lambda i: parts(i)[0] + 1)]),
            Stage("ast", "sync", f_ast, inc=16,
                  active=lambda i: parts(i)[1] == H - 1 and parts(i)[2] == SUB - 1,
                  deps=[("woadd", lambda i: (parts(i)[0] + 1) * 8)]),
        ]
        smap = run_pipeline(nc, stages, NU * SUB, f"a{s}at")
        store_last = smap["ast"]

    with nc.Block() as blk:
        @blk.sync
        def _(sy):
            sy.wait_ge(store_last.ctr.sem, store_last.base + store_last.total * 16)
    return c


LOCAL_EXTRA = [("w_q", (D, D), F32), ("w_k", (D, D), F32), ("w_v", (D, D), F32),
               ("c_mask_n", (128, 64), F32), ("c_mask_1", (128, 64), F32),
               ("c_mask_0", (128, 64), F32)]


def build_local_program(upto=4):
    c = new_program("attn_local", LOCAL_EXTRA)
    c.cur_xin, c.cur_xout = c.xin, c.xout
    aa_reset(c)
    emit_local(c, lambda n: c.inputs[n], "lc_", upto)
    return c


def emit_local(c, gi, pfx, upto=4):
    nc = c.nc

    xt = aa(c, "a_xt", [128, 2 * DC * TT], F32)
    xn = aa(c, "a_xn", [128, DC * SEQ], BF16)
    qf = aa(c, "a_q", [128, DC * SEQ], BF16)
    kf = aa(c, "a_k", [128, DC * SEQ], BF16)
    c._lns_donor = qf
    vt = aa(c, "a_v", [128, KC * H * VW], BF16)
    afm = xn[:, 8192:8192 + DC * TT]
    atm = xn[:, 8192 + DC * TT: 8192 + DC * TT + D]
    strip = aa(c, "a_sstrip", [128, 2 * 64], BF16)
    tvec = aa(c, "a_tvec", [128, 2 * VW], F32)
    rdiv = aa(c, "a_rdiv", [128, 4], F32)
    onesw = aa(c, "a_onesw", [128, H], BF16)
    wring = aa(c, "a_wring", [128, 2 * DC * 128], BF16)
    wv8 = aa(c, "a_wv8", [128, DC * D], BF16)
    maskn = aa(c, "a_maskn", [128, 64], BF16)
    mask1 = aa(c, "a_mask1", [128, 64], BF16)
    mask0 = aa(c, "a_mask0", [128, 64], BF16)

    wl = cpool(c, pfx + "wl")
    hpre = list(c.h_pre)
    with nc.Block() as blk:
        @blk.gpsimd
        def _(g):
            for ctr, v in hpre:
                g.wait_ge(ctr.sem, v)
            wl.inc(g.dma_start(out=wv8[:].rearrange("p (kc m) -> p kc m", kc=DC),
                               in_=gi("w_v").rearrange("(kc p) m -> p kc m", p=128)), 16)
            wl.inc(g.dma_start(out=maskn[:], in_=gi("c_mask_n")), 16)
            wl.inc(g.dma_start(out=mask1[:], in_=gi("c_mask_1")), 16)
            wl.inc(g.dma_start(out=mask0[:], in_=gi("c_mask_0")), 16)
            wl.inc(g.dma_start(out=onesw[:], in_=c.onesw_in), 16)

        @blk.vector
        def _(v):
            v.wait_ge(c.ld.sem, c.consts_ready[1])
            v.tensor_copy(out=strip[:, 0:64], in_=c.zeros[:, 0:64])
            v.tensor_copy(out=strip[:, 64:128], in_=c.zeros[:, 0:64])
    WL = wl.n

    store_last = None
    for s in range(NSEQ):
        seq_tiles = [s * TPS + k for k in range(TPS)]

        def xn_dst(k, dc):
            return xn[:, (dc * TPS + k) * TT:(dc * TPS + k + 1) * TT]

        def xns(dc, col, w):
            return xn[:, dc * SEQ + col: dc * SEQ + col + w]

        pre_load = list(hpre)
        if store_last is not None:
            pre_load = [(store_last.ctr, store_last.base + store_last.total * 16)]
        ln_pipeline(c, f"a{s}ln", seq_tiles, xt, xn_dst, pre_load=pre_load)

        def f_wld(g, i):
            oc = i // 4
            nm = "w_q" if oc < 8 else "w_k"
            return g.dma_start(
                out=wring[:, (oc % 2) * DC * 128:((oc % 2) + 1) * DC * 128]
                    .rearrange("p (kc m) -> p kc m", kc=DC),
                in_=gi(nm).rearrange("(kc p) m -> p kc m", p=128)
                    [:, :, (oc % 8) * 128:((oc % 8) + 1) * 128])

        def f_qkmm(te, i):
            oc, qt = divmod(i, 4)
            last = None
            for kc in range(DC):
                last = te.matmul(
                    c.pb[4 + i % 2][:],
                    wring[:, ((oc % 2) * DC + kc) * 128:((oc % 2) * DC + kc + 1) * 128],
                    xns(kc, qt * TT, TT), start=(kc == 0), stop=(kc == DC - 1))
            return last

        def f_qkev(sc, i):
            oc, qt = divmod(i, 4)
            dst = (qf if oc < 8 else kf)
            o = oc % 8
            return sc.activation(out=dst[:, o * SEQ + qt * TT: o * SEQ + (qt + 1) * TT],
                                 in_=c.pb[4 + i % 2][:], func=AF.Copy)

        run_pipeline(nc, [
            Stage("wld", "gpsimd", f_wld, inc=16, active=lambda i: i % 4 == 0,
                  deps=[("qkmm", lambda i: (i // 4 - 1) * 4)]),
            Stage("qkmm", "tensor", f_qkmm,
                  deps=[("wld", lambda i: i // 4 + 1), ("qkev", lambda i: i - 1)]),
            Stage("qkev", "scalar", f_qkev, deps=[("qkmm", lambda i: i + 1)]),
        ], 64, f"l{s}qk")

        def f_vmm(te, i):
            tc, half = divmod(i, 2)
            last = None
            for cc in range(DC):
                last = te.matmul(c.pb[4 + i % 2][:], xns(cc, tc * 128, 128),
                                 wv8[:, cc * D + half * 512: cc * D + (half + 1) * 512],
                                 start=(cc == 0), stop=(cc == DC - 1))
            return last

        def f_vev(v, i):
            tc, half = divmod(i, 2)
            st0 = tc * H * VW + half * 8 * VW
            dst = vt[:, st0: st0 + 8 * VW].rearrange("p (h w) -> p h w", w=VW)[:, :, 0:64]
            return v.tensor_copy(out=dst, in_=c.pb[4 + i % 2][:].rearrange(
                "p (h e) -> p h e", e=64))

        def f_vones(v, i):
            tc = i // 2
            dst = vt[:, tc * H * VW: (tc + 1) * H * VW].rearrange(
                "p (h w) -> p h w", w=VW)[:, :, 64:65]
            return v.tensor_copy(out=dst, in_=onesw[:].rearrange("p (h o) -> p h o", o=1))

        run_pipeline(nc, [
            Stage("vmm", "tensor", f_vmm, deps=[("vev", lambda i: i - 1)],
                  pre=[(wl, WL)]),
            Stage("vev", "vector", f_vev, deps=[("vmm", lambda i: i + 1)]),
            Stage("vones", "vector", f_vones, active=lambda i: i % 2 == 1,
                  pre=[c.consts_ready, (wl, WL)]),
        ], 32, f"l{s}v")

        SUB = 6
        NU = KC * H

        def parts(i):
            u, sub = divmod(i, SUB)
            return u // H, u % H, sub, u

        def f_xld(sy, i):
            t1 = parts(i)[0] // 4
            return sy.dma_start(
                out=xt[:, (t1 % 2) * DC * TT:((t1 % 2) + 1) * DC * TT]
                    .rearrange("p (c t) -> p c t", c=DC),
                in_=xin_tile(c, s * TPS + t1))

        def f_scmm(te, i):
            t1, h, sub, u = parts(i)
            cch = 2 * t1 + sub
            r0, hc = 64 * (h % 2), h // 2
            bank = c.pb[4 + sub]
            mms = []
            if cch > 0:
                rA = (64 * (cch - 1)) % 128
                mms.append((bank[rA:rA + 64, 0:64],
                            kf[r0:r0 + 64, hc * SEQ + 64 * (cch - 1): hc * SEQ + 64 * cch]))
            rB = (64 * cch) % 128
            mms.append((bank[rB:rB + 64, 0:64],
                        kf[r0:r0 + 64, hc * SEQ + 64 * cch: hc * SEQ + 64 * (cch + 1)]))
            last = None
            for out, l in mms:
                last = te.matmul(out, l,
                                 qf[r0:r0 + 64, hc * SEQ + 64 * cch: hc * SEQ + 64 * (cch + 1)],
                                 start=True, stop=True)
            return last

        def f_scev(sc, i):
            t1, h, sub, u = parts(i)
            cch = 2 * t1 + sub
            bank = c.pb[4 + sub]
            sl = slice(sub * 64, (sub + 1) * 64)
            if cch == 0:
                return sc.activation(out=strip[0:64, sl], in_=bank[0:64, 0:64],
                                     func=AF.Exp, scale=0.125)
            return sc.activation(out=strip[:, sl], in_=bank[0:128, 0:64],
                                 func=AF.Exp, scale=0.125)

        def f_scmask(v, i):
            t1, h, sub, u = parts(i)
            cch = 2 * t1 + sub
            m = mask0 if cch == 0 else (maskn if cch % 2 == 0 else mask1)
            sl = slice(sub * 64, (sub + 1) * 64)
            return v.tensor_mul(out=strip[:, sl], in0=strip[:, sl], in1=m[:])

        def nadd_before(u, j):
            tot = 2 * u + j
            if u < H:
                zeros = u + (1 if j == 1 else 0)
            else:
                zeros = H
            return tot - zeros

        def f_pvmm(te, i):
            t1, h, sub, u = parts(i)
            j = sub - 2
            cch = 2 * t1 + j
            sl = slice(j * 64, j * 64 + 64)
            rq = (cch % 2) * 64
            last = None
            if cch > 0:
                tcA, rA = (cch - 1) // 2, (64 * (cch - 1)) % 128
                last = te.matmul(c.pb[6][rq:rq + 64, 0:VW], strip[rA:rA + 64, sl],
                                 vt[rA:rA + 64,
                                    tcA * H * VW + h * VW: tcA * H * VW + (h + 1) * VW],
                                 start=True, stop=True)
            tcB, rB = cch // 2, (64 * cch) % 128
            last = te.matmul(c.pb[0][rq:rq + 64, 0:VW], strip[rB:rB + 64, sl],
                             vt[rB:rB + 64,
                                tcB * H * VW + h * VW: tcB * H * VW + (h + 1) * VW],
                             start=True, stop=True)
            return last

        def f_pvev(sc, i):
            t1, h, sub, u = parts(i)
            j = sub - 2
            rq = ((2 * t1 + j) % 2) * 64
            return sc.activation(out=tvec[rq:rq + 64, j * VW:(j + 1) * VW],
                                 in_=c.pb[0][rq:rq + 64, 0:VW], func=AF.Copy)

        def f_pvadd(v, i):
            t1, h, sub, u = parts(i)
            j = sub - 2
            rq = ((2 * t1 + j) % 2) * 64
            sl = slice(j * VW, (j + 1) * VW)
            return v.tensor_add(out=tvec[rq:rq + 64, sl], in0=tvec[rq:rq + 64, sl],
                                in1=c.pb[6][rq:rq + 64, 0:VW])

        def f_pvrec(v, i):
            t1, h, sub, u = parts(i)
            j = sub - 2
            rq = ((2 * t1 + j) % 2) * 64
            return v.reciprocal(out=rdiv[rq:rq + 64, j:j + 1],
                                in_=tvec[rq:rq + 64, j * VW + 64: j * VW + 65])

        def f_pvdiv(sc, i):
            t1, h, sub, u = parts(i)
            j = sub - 2
            rq = ((2 * t1 + j) % 2) * 64
            return sc.activation(out=atm[rq:rq + 64, h * 64:(h + 1) * 64],
                                 in_=tvec[rq:rq + 64, j * VW: j * VW + 64],
                                 func=AF.Copy, scale=rdiv[rq:rq + 64, j:j + 1])

        def f_eptr(te, i):
            last = None
            for dc in range(DC):
                last = te.transpose(c.pbt[:, dc * 128:(dc + 1) * 128],
                                    atm[:, dc * 128:(dc + 1) * 128], c.ident[:])
            return last

        def f_epcp(v, i):
            t1 = parts(i)[0]
            last = None
            for dc in range(DC):
                last = v.tensor_copy(
                    out=afm[:, dc * TT + (t1 % 4) * 128: dc * TT + (t1 % 4) * 128 + 128],
                    in_=c.pbt[:, dc * 128:(dc + 1) * 128])
            return last

        def f_resid(g, i):
            t1 = parts(i)[0] // 4
            last = None
            for dc in range(DC):
                dst = xt[:, ((t1 % 2) * DC + dc) * TT:((t1 % 2) * DC + dc + 1) * TT]
                last = g.tensor_add(out=dst, in0=dst,
                                    in1=afm[:, dc * TT:(dc + 1) * TT])
            return last

        def f_ast(sy, i):
            t1 = parts(i)[0] // 4
            return sy.dma_start(
                out=xout_tile(c, s * TPS + t1),
                in_=xt[:, (t1 % 2) * DC * TT:((t1 % 2) + 1) * DC * TT]
                    .rearrange("p (c t) -> p c t", c=DC))

        stages = [
            Stage("xld", "sync", f_xld, inc=16,
                  active=lambda i: parts(i)[0] % 4 == 0 and parts(i)[1] == 0
                                   and parts(i)[2] == 0,
                  deps=[("ast", lambda i: parts(i)[0] // 4 - 1)]),
            Stage("scmm", "tensor", f_scmm, active=lambda i: parts(i)[2] < 2,
                  deps=[("scev", lambda i: 2 * parts(i)[3] + parts(i)[2] - 1)]),
            Stage("scev", "scalar", f_scev, active=lambda i: parts(i)[2] < 2,
                  deps=[("scmm", lambda i: 2 * parts(i)[3] + parts(i)[2] + 1),
                        ("pvmm", lambda i: 2 * parts(i)[3] + parts(i)[2] - 2)],
                  pre=[(wl, WL)]),
            Stage("scmask", "vector", f_scmask, active=lambda i: parts(i)[2] < 2,
                  deps=[("scev", lambda i: 2 * parts(i)[3] + parts(i)[2] + 1)],
                  pre=[(wl, WL)]),
            Stage("pvmm", "tensor", f_pvmm,
                  active=lambda i: 2 <= parts(i)[2] < 4,
                  deps=[("scmask", lambda i: 2 * parts(i)[3] + (parts(i)[2] - 2) + 1),
                        ("pvev", lambda i: 2 * parts(i)[3] + (parts(i)[2] - 2)),
                        ("pvadd", lambda i: nadd_before(parts(i)[3], parts(i)[2] - 2))]),
            Stage("pvev", "scalar", f_pvev,
                  active=lambda i: 2 <= parts(i)[2] < 4,
                  deps=[("pvmm", lambda i: 2 * parts(i)[3] + (parts(i)[2] - 2) + 1),
                        ("pvdiv", lambda i: 2 * parts(i)[3] + (parts(i)[2] - 2) - 1)]),
            Stage("pvadd", "vector", f_pvadd,
                  active=lambda i: 2 <= parts(i)[2] < 4
                                   and 2 * parts(i)[0] + parts(i)[2] - 2 > 0,
                  deps=[("pvev", lambda i: 2 * parts(i)[3] + (parts(i)[2] - 2) + 1)]),
            Stage("pvrec", "vector", f_pvrec,
                  active=lambda i: 2 <= parts(i)[2] < 4,
                  deps=[("pvev", lambda i: 2 * parts(i)[3] + (parts(i)[2] - 2) + 1),
                        ("pvadd", lambda i: nadd_before(parts(i)[3], parts(i)[2] - 2)
                         + (1 if 2 * parts(i)[0] + parts(i)[2] - 2 > 0 else 0))]),
            Stage("pvdiv", "scalar", f_pvdiv,
                  active=lambda i: 2 <= parts(i)[2] < 4,
                  deps=[("pvrec", lambda i: 2 * parts(i)[3] + (parts(i)[2] - 2) + 1),
                        ("eptr", lambda i: parts(i)[0])]),
            Stage("eptr", "tensor", f_eptr,
                  active=lambda i: parts(i)[1] == H - 1 and parts(i)[2] == 4,
                  deps=[("pvdiv", lambda i: 2 * (parts(i)[3] + 1)),
                        ("epcp", lambda i: parts(i)[0])],
                  pre=[c.consts_ready]),
            Stage("epcp", "vector", f_epcp,
                  active=lambda i: parts(i)[1] == H - 1 and parts(i)[2] == 4,
                  deps=[("eptr", lambda i: parts(i)[0] + 1),
                        ("resid", lambda i: parts(i)[0] // 4)]),
            Stage("resid", "gpsimd", f_resid,
                  active=lambda i: parts(i)[0] % 4 == 3 and parts(i)[1] == H - 1
                                   and parts(i)[2] == 5,
                  deps=[("epcp", lambda i: parts(i)[0] + 1),
                        ("xld", lambda i: parts(i)[0] // 4 + 1)]),
            Stage("ast", "sync", f_ast, inc=16,
                  active=lambda i: parts(i)[0] % 4 == 3 and parts(i)[1] == H - 1
                                   and parts(i)[2] == 5,
                  deps=[("resid", lambda i: parts(i)[0] // 4 + 1)]),
        ]
        if upto < 4:
            keep = {1: ["xld", "ast"], 2: ["xld", "scmm", "scev", "scmask", "ast"],
                    3: ["xld", "scmm", "scev", "scmask", "pvmm", "pvrec", "pvdiv",
                        "ast"]}[upto]
            stages = [st for st in stages if st.name in keep]
            lastc = {1: "xld", 2: "scev", 3: "pvdiv"}[upto]
            for st in stages:
                st.deps = [dp for dp in st.deps
                           if not (isinstance(dp[0], str) and
                                   not any(s2.name == dp[0] for s2 in stages))]
                if st.name == "ast":
                    if upto == 1:
                        st.deps = [("xld", lambda i: parts(i)[0] // 4 + 1)]
                    else:
                        st.deps = [("xld", lambda i: parts(i)[0] // 4 + 1),
                                   (lastc, lambda i: 1)]
        smap = run_pipeline(nc, stages, NU * SUB, f"l{s}at")
        store_last = smap["ast"]

    with nc.Block() as blk:
        @blk.sync
        def _(sy):
            sy.wait_ge(store_last.ctr.sem, store_last.base + store_last.total * 16)
    c.h_pre = [(store_last.ctr, store_last.base + store_last.total * 16)]
    return c


def prep_local(inputs):
    wq = np.ascontiguousarray(np.asarray(inputs["L0_attn_a_Wq"], np.float32))
    wk = np.ascontiguousarray(np.asarray(inputs["L0_attn_a_Wk"], np.float32))
    wv = np.ascontiguousarray(np.asarray(inputs["L0_attn_a_Wv"], np.float32))
    tri = np.triu(np.ones((64, 64), np.float32))
    mn = np.ones((128, 64), np.float32); mn[0:64] = tri
    m1 = np.ones((128, 64), np.float32); m1[64:128] = tri
    m0 = np.zeros((128, 64), np.float32); m0[0:64] = tri
    return {"w_q": wq, "w_k": wk, "w_v": wv, "c_mask_n": mn, "c_mask_1": m1,
            "c_mask_0": m0}


NE = 10
HID = 2048
HC = HID // 128


MOE_EXTRA = [("rwT", (D, 8), F32), ("ebias", (128, 8), F32),
             ("esel", (8, 8 * 128), F32),
             ("w1T", (NE, D, HID), F32), ("w3T", (NE, D, HID), F32),
             ("w2T", (NE, HID, D), F32)]


def build_moe_program():
    c = new_program("moe", MOE_EXTRA)
    c.cur_xin, c.cur_xout = c.xin, c.xout
    aa_reset(c)
    emit_moe(c, lambda n: c.inputs[n], "mo_")
    return c


def emit_moe(c, gi, pfx):
    nc = c.nc

    GT = 1024
    NG = T // GT
    GTILES = GT // TT
    xres = aa(c, "m_x", [128, DC * GT], F32)
    xn = aa(c, "m_xn", [128, DC * GT], BF16)
    ht = aa(c, "m_ht", [128, HC * GT], BF16)
    w2r = aa(c, "m_w2", [128, 2 * HC * D], BF16)
    wring = aa(c, "m_wring", [128, 4 * DC * 128], BF16)
    cbt = aa(c, "m_cbt", [8, GT], BF16)
    cbc = aa(c, "m_cbc", [128, GT], BF16)
    rwsb = aa(c, "m_rw", [128, DC * 8], BF16)
    c._lns_donor = w2r
    ebias = aa(c, "m_eb", [128, 8], F32)
    esel = aa(c, "m_esel", [8, 8 * 128], BF16)
    lg = aa(c, "m_lg", [128, 8], F32)
    lm = aa(c, "m_lm", [128, 8], F32)
    d1 = aa(c, "m_d1", [128, 8], F32)
    eq1 = aa(c, "m_eq1", [128, 8], F32)
    eq2 = aa(c, "m_eq2", [128, 8], F32)
    ctm = aa(c, "m_ctm", [128, 8], BF16)
    mcol = aa(c, "m_mcol", [128, 4], F32)
    s1t = aa(c, "m_s1", [128, 2 * TT], BF16)
    t2t = aa(c, "m_t2", [128, 2 * TT], BF16)

    wl = cpool(c, pfx + "wl")
    hpre = list(c.h_pre)
    with nc.Block() as blk:
        @blk.gpsimd
        def _(g):
            for ctr, v in hpre:
                g.wait_ge(ctr.sem, v)
            wl.inc(g.dma_start(out=rwsb[:].rearrange("p (kc e) -> p kc e", kc=DC),
                               in_=gi("rwT").rearrange("(kc p) e -> p kc e", p=128)), 16)
            wl.inc(g.dma_start(out=ebias[:], in_=gi("ebias")), 16)
            wl.inc(g.dma_start(out=esel[:], in_=gi("esel")), 16)
    WL = wl.n
    pool = {nm: cpool(c, pfx + nm) for nm in
            ("cbm", "cb", "w2l", "wld13", "wld3", "amm", "asilu", "amul", "bmm", "badd",
             "rmm", "rv1", "rs1", "rv2", "rs2", "rv3", "rs3", "rs4", "rv4",
             "rs5", "rv5", "rtr", "rcp")}

    store_done = None
    for grp in range(NG):
        seq_tiles = [grp * GTILES + k for k in range(GTILES)]

        def xn_dst(k, dc):
            return xn[:, (dc * GTILES + k) * TT:(dc * GTILES + k + 1) * TT]

        pre_load = list(hpre)
        if store_done is not None:
            pre_load = [(store_done.ctr, store_done.base + store_done.total * 16)]
        ln_pipeline_resident(c, f"m{grp}ln", seq_tiles, xres, xn_dst, pre_load, gt=GT)

        def xns(dc, col, w):
            return xn[:, dc * GT + col: dc * GT + col + w]

        def f_rmm(te, i):
            last = None
            for cc in range(DC):
                last = te.matmul(c.pb[4][0:128, 0:8], xns(cc, i * 128, 128),
                                 rwsb[:, cc * 8:(cc + 1) * 8],
                                 start=(cc == 0), stop=(cc == DC - 1))
            return last

        def f_rv1(v, i):
            v.tensor_add(out=lg[:], in0=c.pb[4][0:128, 0:8], in1=ebias[:])
            v.drain()
            return v.reduce_max(out=mcol[:, 0:1], in_=lg[:], axis=AX, negate=True)

        def f_rs1(sc, i):
            return sc.activation(out=d1[:], in_=lg[:], func=AF.Identity,
                                 bias=mcol[:, 0:1], scale=1.0)

        def f_rv2(v, i):
            v.tensor_scalar(out=eq1[:], in0=d1[:], scalar1=0.0, scalar2=None,
                            op0=ALU.is_ge)
            v.drain()
            v.scalar_tensor_tensor(out=lm[:], in0=eq1[:], scalar=-1e30, in1=lg[:],
                                   op0=ALU.mult, op1=ALU.add)
            v.drain()
            return v.reduce_max(out=mcol[:, 1:2], in_=lm[:], axis=AX, negate=True)

        def f_rs2(sc, i):
            return sc.activation(out=d1[:], in_=lm[:], func=AF.Identity,
                                 bias=mcol[:, 1:2], scale=1.0)

        def f_rv3(v, i):
            v.tensor_scalar(out=eq2[:], in0=d1[:], scalar1=0.0, scalar2=None,
                            op0=ALU.is_ge)
            return v.tensor_sub(out=mcol[:, 2:3], in0=mcol[:, 1:2], in1=mcol[:, 0:1])

        def f_rs3(sc, i):
            return sc.activation(out=mcol[:, 3:4], in_=mcol[:, 2:3], func=AF.Sigmoid)

        def f_rs4(sc, i):
            sc.activation(out=eq1[:], in_=eq1[:], func=AF.Copy, scale=mcol[:, 3:4])
            return sc.drain()

        def f_rv4(v, i):
            return v.tensor_scalar(out=mcol[:, 2:3], in0=mcol[:, 3:4], scalar1=-1.0,
                                   scalar2=1.0, op0=ALU.mult, op1=ALU.add)

        def f_rs5(sc, i):
            sc.activation(out=eq2[:], in_=eq2[:], func=AF.Copy, scale=mcol[:, 2:3])
            return sc.drain()

        def f_rv5(v, i):
            return v.tensor_add(out=ctm[:], in0=eq1[:], in1=eq2[:])

        def f_rtr(te, i):
            return te.transpose(c.pbt[0:8, 0:128], ctm[:], c.ident[:])

        def f_rcp(v, i):
            return v.tensor_copy(out=cbt[:, i * 128:(i + 1) * 128],
                                 in_=c.pbt[0:8, 0:128])

        one = lambda i: i + 1
        run_pipeline(nc, [
            Stage("rmm", "tensor", f_rmm, pre=[(wl, WL)],
                  deps=[("rv1", lambda i: i)], ctr=pool["rmm"]),
            Stage("rv1", "vector", f_rv1, deps=[("rmm", one), ("rs1", lambda i: i)],
                  pre=[(wl, WL)], ctr=pool["rv1"]),
            Stage("rs1", "scalar", f_rs1, deps=[("rv1", one)], ctr=pool["rs1"]),
            Stage("rv2", "vector", f_rv2, deps=[("rs1", one)], ctr=pool["rv2"]),
            Stage("rs2", "scalar", f_rs2, deps=[("rv2", one)], ctr=pool["rs2"]),
            Stage("rv3", "vector", f_rv3, deps=[("rs2", one)], ctr=pool["rv3"]),
            Stage("rs3", "scalar", f_rs3, deps=[("rv3", one)], ctr=pool["rs3"]),
            Stage("rs4", "scalar", f_rs4, deps=[("rs3", one), ("rv5", lambda i: i)],
                  ctr=pool["rs4"]),
            Stage("rv4", "vector", f_rv4, deps=[("rs3", one)], ctr=pool["rv4"]),
            Stage("rs5", "scalar", f_rs5, deps=[("rv4", one), ("rv5", lambda i: i)],
                  ctr=pool["rs5"]),
            Stage("rv5", "vector", f_rv5, deps=[("rs4", one), ("rs5", one)],
                  ctr=pool["rv5"]),
            Stage("rtr", "tensor", f_rtr, deps=[("rv5", one), ("rcp", lambda i: i)],
                  pre=[c.consts_ready], ctr=pool["rtr"]),
            Stage("rcp", "vector", f_rcp, deps=[("rtr", one)], ctr=pool["rcp"]),
        ], GT // 128, f"m{grp}r")

        NTTG = GT // TT
        last_badd = None
        for e in range(NE):
            cbm, cb = pool["cbm"], pool["cb"]
            cbm_b, cb_b = cbm.n, cb.n
            if e < 8:
                with nc.Block() as blk:
                    @blk.tensor
                    def _(te, _e=e):
                        for tt in range(NTTG):
                            if tt > 0:
                                te.wait_ge(cb.sem, cb_b + tt)
                            cbm.inc(te.matmul(
                                c.pb[6][:], esel[0:8, _e * 128:(_e + 1) * 128],
                                cbt[0:8, tt * TT:(tt + 1) * TT],
                                start=True, stop=True), 1)

                    @blk.vector
                    def _(v):
                        for tt in range(NTTG):
                            v.wait_ge(cbm.sem, cbm_b + tt + 1)
                            cb.inc(v.tensor_copy(
                                out=cbc[:, tt * TT:(tt + 1) * TT], in_=c.pb[6][:]), 1)
            CB = cb.n

            NA = HC * NTTG

            def mk_wld(j, nm):
                def f(g, i, _e=e):
                    hc = i // NTTG
                    return g.dma_start(
                        out=wring[:, ((hc % 2) * 2 + j) * DC * 128:
                                  ((hc % 2) * 2 + j + 1) * DC * 128]
                            .rearrange("p (kc m) -> p kc m", kc=DC),
                        in_=gi(nm)[_e].rearrange("(kc p) m -> p kc m", p=128)
                            [:, :, hc * 128:(hc + 1) * 128])
                return f
            f_wld1 = mk_wld(0, "w1T")
            f_wld3 = mk_wld(1, "w3T")

            def f_w2ld(g, i, _e=e):
                hc = i // NTTG
                base = (_e % 2) * HC * D
                return g.dma_start(
                    out=w2r[:, base + hc * D: base + (hc + 1) * D]
                        .rearrange("p (kc m) -> p kc m", kc=1),
                    in_=gi("w2T")[_e].rearrange("(kc p) m -> p kc m", p=128)
                        [:, hc:hc + 1, :])

            def f_amm(te, i):
                hc, tt = divmod(i, NTTG)
                b1, b3 = [4, 6][i % 2], [5, 0][i % 2]
                for kc in range(DC):
                    te.matmul(c.pb[b1][:],
                              wring[:, ((hc % 2) * 2 + 0) * DC * 128 + kc * 128:
                                    ((hc % 2) * 2 + 0) * DC * 128 + (kc + 1) * 128],
                              xns(kc, tt * TT, TT), start=(kc == 0), stop=(kc == DC - 1))
                last = None
                for kc in range(DC):
                    last = te.matmul(c.pb[b3][:],
                                     wring[:, ((hc % 2) * 2 + 1) * DC * 128 + kc * 128:
                                           ((hc % 2) * 2 + 1) * DC * 128 + (kc + 1) * 128],
                                     xns(kc, tt * TT, TT),
                                     start=(kc == 0), stop=(kc == DC - 1))
                return last

            def f_asilu(sc, i):
                return sc.activation(out=s1t[:, (i % 2) * TT:(i % 2 + 1) * TT],
                                     in_=c.pb[[4, 6][i % 2]][:], func=AF.Silu)

            def f_amul(v, i, _e=e):
                hc, tt = divmod(i, NTTG)
                b3 = [5, 0][i % 2]
                sl = slice((i % 2) * TT, (i % 2 + 1) * TT)
                if _e < 8:
                    v.tensor_mul(out=t2t[:, sl], in0=c.pb[b3][:],
                                 in1=cbc[:, tt * TT:(tt + 1) * TT])
                else:
                    v.tensor_copy(out=t2t[:, sl], in_=c.pb[b3][:])
                v.drain()
                return v.tensor_mul(out=ht[:, (hc * NTTG + tt) * TT:
                                           (hc * NTTG + tt + 1) * TT],
                                    in0=s1t[:, sl], in1=t2t[:, sl])

            stA = [
                Stage("wld1", "gpsimd", f_wld1, inc=16,
                      active=lambda i: i % NTTG == 0,
                      deps=[("amm", lambda i: (i // NTTG - 1) * NTTG)],
                      pre=([(cb, CB)] if e < 8 else [(wl, WL)]),
                      ctr=pool["wld13"]),
                Stage("wld3", "gpsimd", f_wld3, inc=16,
                      active=lambda i: i % NTTG == 0,
                      deps=[("amm", lambda i: (i // NTTG - 1) * NTTG)],
                      ctr=pool["wld3"]),
                Stage("w2ld", "gpsimd", f_w2ld, inc=16,
                      active=lambda i: i % NTTG == 0,
                      pre=[(pool["bmm"], max(0, pool["bmm"].n - NTTG * DC))],
                      ctr=pool["w2l"]),
                Stage("amm", "tensor", f_amm,
                      deps=[("wld1", lambda i: i // NTTG + 1),
                            ("wld3", lambda i: i // NTTG + 1),
                            ("asilu", lambda i: i - 1), ("amul", lambda i: i - 1)],
                      ctr=pool["amm"]),
                Stage("asilu", "scalar", f_asilu, deps=[("amm", lambda i: i + 1)],
                      ctr=pool["asilu"]),
                Stage("amul", "vector", f_amul,
                      deps=[("amm", lambda i: i + 1), ("asilu", lambda i: i + 1)],
                      ctr=pool["amul"]),
            ]
            smA = run_pipeline(nc, stA, NA, f"m{grp}e{e}a")
            w2l = pool["w2l"]
            W2L = w2l.n

            NB = NTTG * DC

            def f_bmm(te, i, _e=e):
                tt, dc = divmod(i, DC)
                base = (_e % 2) * HC * D
                last = None
                for hc in range(HC):
                    last = te.matmul(c.pb[dc % 4][:],
                                     w2r[:, base + hc * D + dc * 128:
                                         base + hc * D + (dc + 1) * 128],
                                     ht[:, (hc * NTTG + tt) * TT:(hc * NTTG + tt + 1) * TT],
                                     start=(hc == 0), stop=(hc == HC - 1))
                return last

            def f_badd(v, i):
                tt, dc = divmod(i, DC)
                dst = xres[:, dc * GT + tt * TT: dc * GT + (tt + 1) * TT]
                return v.tensor_add(out=dst, in0=dst, in1=c.pb[dc % 4][:])

            smB = run_pipeline(nc, [
                Stage("bmm", "tensor", f_bmm, pre=[(w2l, W2L)],
                      deps=[("badd", lambda i: i - 3)], ctr=pool["bmm"]),
                Stage("badd", "vector", f_badd, deps=[("bmm", lambda i: i + 1)],
                      ctr=pool["badd"]),
            ], NB, f"m{grp}e{e}b")
            last_badd = smB["badd"]

        st = cpool(c, pfx + "st")
        with nc.Block() as blk:
            @blk.sync
            def _(sy, _g=grp):
                st_b = st.n
                for k in range(GTILES):
                    if k > 0:
                        sy.wait_ge(st.sem, st_b + k * 16)
                    i = sy.dma_start(
                        out=xout_tile(c, _g * GTILES + k),
                        in_=xres[:, 0:DC * GT].rearrange("p (c t) -> p c t", c=DC)
                            [:, :, k * TT:(k + 1) * TT])
                    st.inc(i, 16)
                sy.wait_ge(st.sem, st.n)
        store_done = Stage("dummy", "sync", lambda e, i: None)
        store_done.ctr = st
        store_done.base = st.n - GTILES * 16
        store_done.total = GTILES
        store_done.inc_amt = 16

    c.h_pre = [(store_done.ctr, store_done.base + store_done.total * 16)]
    return c


def ln_pipeline_resident(c, tag, tiles, xres, xn_dst, pre_load=(), gt=SEQ):
    nc = c.nc
    nt = len(tiles)
    xb, xsq, rows, tmp8 = ln_scratch(c)
    R_M, R_V, R_A, R_B, R_T = (k * TT for k in range(5))
    ps_sum, ps_sq = c.pb[0][0:1, :], c.pb[1][0:1, :]
    bA, bB = c.pb[2], c.pb[3]

    def xts(k, dc):
        return xres[:, dc * gt + k * TT: dc * gt + (k + 1) * TT]

    tl_of = lambda i: i // DC
    dc_of = lambda i: i % DC
    w_of = lambda i: i - DC

    def f_load(sy, i):
        k = tl_of(i)
        return sy.dma_start(
            out=xres[:, 0:DC * gt].rearrange("p (c t) -> p c t", c=DC)
                [:, :, k * TT:(k + 1) * TT],
            in_=xin_tile(c, tiles[k]))

    def f_cast(g, i):
        k, dc = tl_of(i), dc_of(i)
        return g.tensor_copy(out=xb[:, (i % 2) * TT:(i % 2 + 1) * TT], in_=xts(k, dc))

    def f_sq(v, i):
        s = slice((i % 2) * TT, (i % 2 + 1) * TT)
        return v.tensor_mul(out=xsq[:, s], in0=xb[:, s], in1=xb[:, s])

    def f_mm2(te, i):
        dc = dc_of(i)
        s = slice((i % 2) * TT, (i % 2 + 1) * TT)
        te.matmul(ps_sum, c.onescol[:, 0:1], xb[:, s],
                  start=(dc == 0), stop=(dc == DC - 1))
        return te.matmul(ps_sq, c.onescol[:, 0:1], xsq[:, s],
                         start=(dc == 0), stop=(dc == DC - 1))

    def f_rowv(v, i):
        v.tensor_scalar(out=rows[:, R_M:R_M + TT], in0=ps_sum, scalar1=1.0 / D,
                        scalar2=None, op0=ALU.mult)
        v.tensor_scalar(out=rows[:, R_T:R_T + TT], in0=ps_sq, scalar1=1.0 / D,
                        scalar2=None, op0=ALU.mult)
        v.drain()
        v.tensor_mul(out=rows[:, R_V:R_V + TT], in0=rows[:, R_M:R_M + TT],
                     in1=rows[:, R_M:R_M + TT])
        v.drain()
        return v.tensor_sub(out=rows[:, R_V:R_V + TT], in0=rows[:, R_T:R_T + TT],
                            in1=rows[:, R_V:R_V + TT])

    def f_rows(sc, i):
        sc.activation(out=rows[:, R_T:R_T + TT], in_=rows[:, R_V:R_V + TT],
                      func=AF.Ln, bias=c.eps[0:1, 0:1], scale=1.0)
        sc.drain()
        return sc.activation(out=rows[:, R_A:R_A + TT], in_=rows[:, R_T:R_T + TT],
                             func=AF.Exp, scale=-0.5)

    def f_rowb(v, i):
        return v.scalar_tensor_tensor(out=rows[:, R_B:R_B + TT],
                                      in0=rows[:, R_M:R_M + TT], scalar=-1.0,
                                      in1=rows[:, R_A:R_A + TT],
                                      op0=ALU.mult, op1=ALU.mult)

    def f_bc(te, i):
        te.matmul(bA[:], c.onesf[:, 0:128], rows[:, R_A:R_A + TT], start=True, stop=True)
        return te.matmul(bB[:], c.onesf[:, 0:128], rows[:, R_B:R_B + TT],
                         start=True, stop=True)

    def f_apply(v, i):
        w = w_of(i)
        k, dc = w // DC, w % DC
        sl = slice((i % 2) * TT, (i % 2 + 1) * TT)
        v.tensor_mul(out=tmp8[:, sl], in0=xts(k, dc), in1=bA[:])
        v.drain()
        return v.tensor_add(out=xn_dst(k, dc), in0=tmp8[:, sl], in1=bB[:])

    N = (nt + 1) * DC
    ld_act = lambda i: i % DC == 0 and i < nt * DC
    st_act = lambda i: i < nt * DC
    ap_act = lambda i: i >= DC
    stages = [
        Stage("load", "sync", f_load, inc=16, active=ld_act, pre=list(pre_load)),
        Stage("cast", "gpsimd", f_cast, active=st_act, pre=[c.consts_ready],
              deps=[("load", lambda i: tl_of(i) + 1), ("mm2", lambda i: i - 1),
                    ("sq", lambda i: i - 1)]),
        Stage("sq", "vector", f_sq, active=st_act, deps=[("cast", lambda i: i + 1)]),
        Stage("mm2", "tensor", f_mm2, active=st_act, pre=[c.consts_ready],
              deps=[("cast", lambda i: i + 1), ("sq", lambda i: i + 1),
                    ("rowv", lambda i: tl_of(i))]),
        Stage("rowv", "vector", f_rowv, active=lambda i: st_act(i) and dc_of(i) == DC - 1,
              deps=[("mm2", lambda i: i + 1)]),
        Stage("rows", "scalar", f_rows, active=lambda i: st_act(i) and dc_of(i) == DC - 1,
              deps=[("rowv", lambda i: tl_of(i) + 1)]),
        Stage("rowb", "vector", f_rowb, active=lambda i: st_act(i) and dc_of(i) == DC - 1,
              deps=[("rows", lambda i: tl_of(i) + 1), ("rowv", lambda i: tl_of(i) + 1)]),
        Stage("bc", "tensor", f_bc, active=lambda i: st_act(i) and dc_of(i) == DC - 1,
              deps=[("rows", lambda i: tl_of(i) + 1), ("rowb", lambda i: tl_of(i) + 1),
                    ("apply", lambda i: tl_of(i) * DC)]),
        Stage("apply", "vector", f_apply, active=ap_act,
              deps=[("bc", lambda i: w_of(i) // DC + 1)]),
    ]
    return run_pipeline(nc, stages, N, tag)


def prep_moe(inputs):
    rw = np.ascontiguousarray(np.asarray(inputs["L1_ffn_router_w"], np.float32).T)
    eb = np.ascontiguousarray(np.broadcast_to(
        np.asarray(inputs["L1_ffn_expert_bias"], np.float32)[None, :], (128, 8)))
    esel = np.zeros((8, 8 * 128), np.float32)
    for e in range(8):
        esel[e, e * 128:(e + 1) * 128] = 1.0
    w1 = np.asarray(inputs["L1_ffn_w1"], np.float32)
    w3 = np.asarray(inputs["L1_ffn_w3"], np.float32)
    w2 = np.asarray(inputs["L1_ffn_w2"], np.float32)
    sw1 = np.asarray(inputs["L1_ffn_sw1"], np.float32)
    sw3 = np.asarray(inputs["L1_ffn_sw3"], np.float32)
    sw2 = np.asarray(inputs["L1_ffn_sw2"], np.float32)
    w1T = np.ascontiguousarray(np.concatenate([w1, sw1], 0).transpose(0, 2, 1))
    w3T = np.ascontiguousarray(np.concatenate([w3, sw3], 0).transpose(0, 2, 1))
    w2T = np.ascontiguousarray(np.concatenate([w2, sw2], 0).transpose(0, 2, 1))
    return {"rwT": rw, "ebias": eb, "esel": esel, "w1T": w1T, "w3T": w3T, "w2T": w2T}


CH = 64
HGT = 1024


GLA_EXTRA = [("wqT", (D, D), F32), ("wkT", (D, D), F32), ("wvT", (D, D), F32),
             ("wgT", (D, D), F32), ("woT", (D, D), F32),
             ("gamma_row", (1, D), F32), ("beta_row", (1, D), F32),
             ("c_triu8", (128, 512), F32)]


def build_gla_program():
    c = new_program("gla", GLA_EXTRA)
    c.cur_xin, c.cur_xout = c.xin, c.xout
    aa_reset(c)
    emit_gla(c, lambda n: c.inputs[n], "gl_")
    return c


def emit_gla(c, gi, pfx):
    nc = c.nc
    NH = T // HGT
    HTILES = HGT // TT
    NTC = HGT // 128
    NC64 = HGT // CH
    HP = 8

    xt = aa(c, "g_xt", [128, 2 * DC * TT], F32)
    xn = aa(c, "g_xn", [128, DC * HGT], BF16)
    qb = aa(c, "g_q", [128, DC * HGT], BF16)
    c._lns_donor = qb
    kb = aa(c, "g_k", [128, DC * HGT], BF16)
    zb = aa(c, "g_z", [128, DC * HGT], BF16)
    vb = aa(c, "g_v", [128, NTC * D], BF16)
    yst = aa(c, "g_y", [128, NTC * D], BF16)
    wv8 = aa(c, "g_wv8", [128, DC * D], BF16)
    wring = aa(c, "g_wring", [128, 2 * DC * 128], BF16)
    woring = aa(c, "g_woring", [128, 2 * DC * 128], BF16)
    gbt = aa(c, "g_gb", [128, D], BF16)
    bbt = aa(c, "g_bb", [128, D], BF16)
    Hst = aa(c, "g_H", [128, HP * CH], BF16)
    W8 = HP * CH
    za = aa(c, "g_za", [128, W8], F32)
    ab = aa(c, "g_ab", [128, W8], F32)
    loga = aa(c, "g_loga", [128, W8], F32)
    cums = aa(c, "g_cums", [128, W8], F32)
    Aex = aa(c, "g_A", [128, W8], F32)
    Enx = aa(c, "g_En", [128, W8], F32)
    alc8 = aa(c, "g_al8", [128, DC], F32)
    qt_ = aa(c, "g_qt", [128, W8], BF16)
    kh_ = aa(c, "g_kh", [128, W8], BF16)
    kbar = aa(c, "g_kbar", [128, W8], BF16)
    smT0 = aa(c, "g_smT0", [128, W8], BF16)
    smT1 = aa(c, "g_smT1", [128, W8], BF16)
    tri8 = aa(c, "g_tri8", [128, W8], BF16)
    ktb = aa(c, "g_ktb", [128, 1024], BF16)
    htmp = aa(c, "g_htmp", [128, W8], F32)
    yfm = xn
    ynt = aa(c, "g_ynt", [128, 2 * D], BF16)
    nrow = aa(c, "g_nrow", [128, 4], F32)
    grow = aa(c, "g_grow", [1, D], F32)
    brow = aa(c, "g_brow", [1, D], F32)

    wl = cpool(c, pfx + "wl")
    hpre = list(c.h_pre)
    with nc.Block() as blk:
        @blk.gpsimd
        def _(g):
            for ctr, v in hpre:
                g.wait_ge(ctr.sem, v)
            wl.inc(g.dma_start(out=wv8[:].rearrange("p (kc m) -> p kc m", kc=DC),
                               in_=gi("wvT").rearrange("(kc p) m -> p kc m", p=128)), 16)
            wl.inc(g.dma_start(out=tri8[:], in_=gi("c_triu8")), 16)
            wl.inc(g.dma_start(out=grow[:], in_=gi("gamma_row")), 16)
            wl.inc(g.dma_start(out=brow[:], in_=gi("beta_row")), 16)

    WL = wl.n

    gbm = cpool(c, pfx + "gbm")
    gb = cpool(c, pfx + "gbc")
    gbm_b, gb_b = gbm.n, gb.n
    with nc.Block() as blk:
        @blk.tensor
        def _(te):
            te.wait_ge(wl.sem, WL)
            te.wait_ge(c.ld.sem, c.consts_ready[1])
            step = 0
            for j, row in ((0, grow), (1, brow)):
                for hh in range(2):
                    if step > 0:
                        te.wait_ge(gb.sem, gb_b + step)
                    gbm.inc(te.matmul(c.pb[2 + j][:, 0:512], c.onesf[:, 0:128],
                                      row[:, hh * 512:(hh + 1) * 512],
                                      start=True, stop=True), 1)
                    step += 1

        @blk.vector
        def _(v):
            step = 0
            for j, dst in ((0, gbt), (1, bbt)):
                for hh in range(2):
                    v.wait_ge(gbm.sem, gbm_b + step + 1)
                    gb.inc(v.tensor_copy(out=dst[:, hh * 512:(hh + 1) * 512],
                                         in_=c.pb[2 + j][:, 0:512]), 1)
                    step += 1
    GB = gb.n

    hz = cpool(c, pfx + "hz")

    def zero_state():
        with nc.Block() as blk:
            @blk.gpsimd
            def _(g):
                for p in range(HP):
                    hz.inc(g.dma_start(out=Hst[:, p * CH:(p + 1) * CH],
                                       in_=c.zeros_in[:, 0:64]), 16)
        return hz.n

    def P(nm):
        return cpool(c, pfx + nm)

    store_done = None
    for half in range(NH):
        tiles = [half * HTILES + k for k in range(HTILES)]

        def xn_dst(k, dc):
            return xn[:, (dc * HTILES + k) * TT:(dc * HTILES + k + 1) * TT]

        def xns(dc, col, w):
            return xn[:, dc * HGT + col: dc * HGT + col + w]

        if half % (NH // NSEQ) == 0:
            HZ = zero_state()
        pre_load = list(hpre)
        if store_done is not None:
            pre_load = [(store_done.ctr, store_done.base + store_done.total * 16)]
        ln_pipeline(c, f"g{half}ln", tiles, xt, xn_dst, pre_load=pre_load)

        NQ = 3 * DC * HTILES

        def f_wld(g, i, _h=half):
            u = i // HTILES
            w, oc = divmod(u, DC)
            nm = ("wqT", "wkT", "wgT")[w]
            return g.dma_start(
                out=wring[:, (u % 2) * DC * 128:((u % 2) + 1) * DC * 128]
                    .rearrange("p (kc m) -> p kc m", kc=DC),
                in_=gi(nm).rearrange("(kc p) m -> p kc m", p=128)
                    [:, :, oc * 128:(oc + 1) * 128])

        def f_pmm(te, i):
            u, tt = divmod(i, HTILES)
            last = None
            for kc in range(DC):
                last = te.matmul(
                    c.pb[4 + i % 2][:],
                    wring[:, ((u % 2) * DC + kc) * 128:((u % 2) * DC + kc + 1) * 128],
                    xns(kc, tt * TT, TT), start=(kc == 0), stop=(kc == DC - 1))
            return last

        def f_pev(sc, i):
            u, tt = divmod(i, HTILES)
            w, oc = divmod(u, DC)
            dst = (qb, kb, zb)[w]
            return sc.activation(out=dst[:, oc * HGT + tt * TT: oc * HGT + (tt + 1) * TT],
                                 in_=c.pb[4 + i % 2][:], func=AF.Copy)

        run_pipeline(nc, [
            Stage("wld", "gpsimd", f_wld, inc=16, active=lambda i: i % HTILES == 0,
                  deps=[("pmm", lambda i: (i // HTILES - 1) * HTILES)],
                  ctr=P("wld"), pre=[(wl, WL)]),
            Stage("pmm", "tensor", f_pmm,
                  deps=[("wld", lambda i: i // HTILES + 1), ("pev", lambda i: i - 1)],
                  ctr=P("pmm")),
            Stage("pev", "scalar", f_pev, deps=[("pmm", lambda i: i + 1)],
                  ctr=P("pev")),
        ], NQ, f"g{half}p")

        def f_vmm(te, i):
            tc, hh = divmod(i, 2)
            last = None
            for cc in range(DC):
                last = te.matmul(c.pb[4 + i % 2][:], xns(cc, tc * 128, 128),
                                 wv8[:, cc * D + hh * 512: cc * D + (hh + 1) * 512],
                                 start=(cc == 0), stop=(cc == DC - 1))
            return last

        def f_vev(v, i):
            tc, hh = divmod(i, 2)
            return v.tensor_copy(out=vb[:, tc * D + hh * 512: tc * D + (hh + 1) * 512],
                                 in_=c.pb[4 + i % 2][:])

        run_pipeline(nc, [
            Stage("vmm", "tensor", f_vmm, deps=[("vev", lambda i: i - 1)],
                  ctr=P("vmm"), pre=[(wl, WL)]),
            Stage("vev", "vector", f_vev, deps=[("vmm", lambda i: i + 1)],
                  ctr=P("vev")),
        ], NTC * 2, f"g{half}v")

        NUG = NC64

        def zv(buf, cch):
            return buf[:].rearrange("p (d t) -> p d t", d=DC)[:, :, cch * CH:(cch + 1) * CH]

        def fl3(buf):
            return buf[:].rearrange("p (d t) -> p d t", d=DC)

        def f_g1(v, i):
            return v.tensor_scalar(out=fl3(za), in0=zv(zb, i), scalar1=0.0,
                                   scalar2=None, op0=ALU.min)

        def f_g2(sc, i):
            sc.activation(out=fl3(ab), in_=zv(zb, i), func=AF.Abs)
            sc.drain()
            sc.activation(out=ab[:], in_=ab[:], func=AF.Exp, scale=-1.0)
            sc.drain()
            return sc.activation(out=ab[:], in_=ab[:], func=AF.Ln,
                                 bias=c.onecolf[:, 0:1], scale=1.0)

        def f_g3(v, i):
            v.tensor_sub(out=loga[:], in0=za[:], in1=ab[:])
            v.drain()
            last = None
            for p in range(HP):
                last = v.tensor_tensor_scan(
                    out=cums[:, p * CH:(p + 1) * CH],
                    data0=loga[:, p * CH:(p + 1) * CH], data1=c.zerosf[:],
                    initial=0.0, op0=ALU.add, op1=ALU.add)
            return last

        def f_g4(sc, i):
            sc.activation(out=Aex[:], in_=cums[:], func=AF.Exp)
            sc.activation(out=Enx[:], in_=cums[:], func=AF.Exp, scale=-1.0)
            sc.drain()
            return sc.activation(
                out=alc8[:],
                in_=cums[:].rearrange("p (d t) -> p d t", d=DC)[:, :, CH - 1:CH],
                func=AF.Exp)

        def f_g5(v, i):
            v.tensor_mul(out=fl3(qt_), in0=zv(qb, i), in1=fl3(Aex))
            return v.tensor_mul(out=fl3(kh_), in0=zv(kb, i), in1=fl3(Enx))

        def f_g6(sc, i):
            last = None
            for p in range(HP):
                last = sc.activation(out=kbar[:, p * CH:(p + 1) * CH],
                                     in_=kh_[:, p * CH:(p + 1) * CH],
                                     func=AF.Copy, scale=alc8[:, p:p + 1])
            return last

        def f_s1(te, i):
            cch = i
            rc = (CH * cch) % 128
            for p in range(HP):
                for hh in range(2):
                    r0 = 64 * hh
                    te.matmul(c.pb[4 + hh][rc:rc + 64, p * CH:(p + 1) * CH],
                              kh_[r0:r0 + 64, p * CH:(p + 1) * CH],
                              qt_[r0:r0 + 64, p * CH:(p + 1) * CH],
                              start=True, stop=True)
            last = None
            for p in range(HP):
                last = te.transpose(c.pbt[rc:rc + 64, p * 128:(p + 1) * 128],
                                    kbar[:, p * CH:(p + 1) * CH], c.ident[:])
            return last

        def f_s2(v, i):
            cch = i
            rc = (CH * cch) % 128
            v.tensor_mul(out=smT0[rc:rc + 64, :], in0=c.pb[4][rc:rc + 64, 0:512],
                         in1=tri8[rc:rc + 64, :])
            v.tensor_mul(out=smT1[rc:rc + 64, :], in0=c.pb[5][rc:rc + 64, 0:512],
                         in1=tri8[rc:rc + 64, :])
            return v.tensor_copy(out=ktb[rc:rc + 64, :], in_=c.pbt[rc:rc + 64, 0:1024])

        def f_y1(te, i):
            cch = i
            rc = (CH * cch) % 128
            tc = cch // 2
            for p in range(HP):
                te.matmul(c.pb[6][rc:rc + 64, p * CH:(p + 1) * CH],
                          smT0[rc:rc + 64, p * CH:(p + 1) * CH],
                          vb[rc:rc + 64, tc * D + (2 * p) * 64: tc * D + (2 * p + 1) * 64],
                          start=True, stop=True)
                te.matmul(c.pb[0][rc:rc + 64, p * CH:(p + 1) * CH],
                          smT1[rc:rc + 64, p * CH:(p + 1) * CH],
                          vb[rc:rc + 64, tc * D + (2 * p + 1) * 64: tc * D + (2 * p + 2) * 64],
                          start=True, stop=True)
                te.matmul(c.pb[1][rc:rc + 64, p * CH:(p + 1) * CH],
                          qt_[0:64, p * CH:(p + 1) * CH],
                          Hst[0:64, p * CH:(p + 1) * CH], start=True, stop=True)
                te.matmul(c.pb[2][rc:rc + 64, p * CH:(p + 1) * CH],
                          qt_[64:128, p * CH:(p + 1) * CH],
                          Hst[64:128, p * CH:(p + 1) * CH], start=True, stop=True)
            last = None
            for p in range(HP):
                for hh in range(2):
                    h = 2 * p + hh
                    last = te.matmul(
                        c.pb[3][hh * 64:(hh + 1) * 64, p * CH:(p + 1) * CH],
                        ktb[rc:rc + 64, p * 128 + hh * 64: p * 128 + hh * 64 + 64],
                        vb[rc:rc + 64, tc * D + h * 64: tc * D + (h + 1) * 64],
                        start=True, stop=True)
            return last
        def f_y2(sc, i):
            cch = i
            rc = (CH * cch) % 128
            tc = cch // 2
            sc.activation(out=yst[rc:rc + 64, tc * D:(tc + 1) * D]
                          .rearrange("p (d e) -> p d e", e=64)[:, 0::2, :],
                          in_=c.pb[6][rc:rc + 64, 0:512], func=AF.Copy)
            return sc.activation(out=yst[rc:rc + 64, tc * D:(tc + 1) * D]
                                 .rearrange("p (d e) -> p d e", e=64)[:, 1::2, :],
                                 in_=c.pb[0][rc:rc + 64, 0:512], func=AF.Copy)

        def f_y3(v, i):
            cch = i
            rc = (CH * cch) % 128
            tc = cch // 2
            dst0 = yst[rc:rc + 64, tc * D:(tc + 1) * D].rearrange(
                "p (d e) -> p d e", e=64)[:, 0::2, :]
            v.tensor_add(out=dst0, in0=dst0, in1=c.pb[1][rc:rc + 64, 0:512]
                         .rearrange("p (d e) -> p d e", e=64))
            dst1 = yst[rc:rc + 64, tc * D:(tc + 1) * D].rearrange(
                "p (d e) -> p d e", e=64)[:, 1::2, :]
            return v.tensor_add(out=dst1, in0=dst1, in1=c.pb[2][rc:rc + 64, 0:512]
                                .rearrange("p (d e) -> p d e", e=64))

        def f_h1(sc, i):
            last = None
            for p in range(HP):
                last = sc.activation(out=htmp[:, p * CH:(p + 1) * CH],
                                     in_=Hst[:, p * CH:(p + 1) * CH],
                                     func=AF.Copy, scale=alc8[:, p:p + 1])
            return last

        def f_h2(v, i):
            return v.tensor_add(out=Hst[:], in0=htmp[:], in1=c.pb[3][:, 0:512])

        one = lambda i: i + 1
        gst = [
            Stage("g1", "vector", f_g1, deps=[("g3", lambda i: i)], ctr=P("g1"),
                  pre=[(hz, HZ)]),
            Stage("g2", "scalar", f_g2, deps=[("g1", one), ("g3", lambda i: i)],
                  ctr=P("g2"), pre=[(wl, WL)]),
            Stage("g3", "vector", f_g3, deps=[("g2", one), ("g4", lambda i: i)],
                  ctr=P("g3")),
            Stage("g4", "scalar", f_g4, deps=[("g3", one), ("g5", lambda i: i),
                                              ("g6", lambda i: i),
                                              ("h1", lambda i: i)], ctr=P("g4")),
            Stage("g5", "vector", f_g5, deps=[("g4", one), ("s1", lambda i: i),
                                              ("y1", lambda i: i)], ctr=P("g5")),
            Stage("g6", "scalar", f_g6, deps=[("g4", one), ("g5", one),
                                              ("s1", lambda i: i)], ctr=P("g6")),
            Stage("s1", "tensor", f_s1, deps=[("g5", one), ("g6", one),
                                              ("s2", lambda i: i)], ctr=P("s1"),
                  pre=[c.consts_ready]),
            Stage("s2", "vector", f_s2, deps=[("s1", one), ("y1", lambda i: i)],
                  ctr=P("s2"), pre=[(wl, WL)]),
            Stage("y1", "tensor", f_y1,
                  deps=[("s2", one), ("g5", one), ("g6", one), ("y2", lambda i: i),
                        ("y3", lambda i: i), ("h2", lambda i: i),
                        ("h1", lambda i: i)],
                  ctr=P("y1")),
            Stage("y2", "scalar", f_y2, deps=[("y1", one)], ctr=P("y2")),
            Stage("y3", "vector", f_y3, deps=[("y1", one), ("y2", one)], ctr=P("y3")),
            Stage("h1", "scalar", f_h1, deps=[("g4", one), ("y1", lambda i: i + 1),
                                              ("h2", lambda i: i)], ctr=P("h1")),
            Stage("h2", "vector", f_h2, deps=[("h1", one), ("y1", one)], ctr=P("h2")),
        ]
        run_pipeline(nc, gst, NUG, f"g{half}c")

        def f_n1(v, i):
            tc = i
            v.reduce_sum(out=nrow[:, 0:1], in_=yst[:, tc * D:(tc + 1) * D], axis=AX)
            v.drain()
            return v.tensor_scalar(out=nrow[:, 1:2], in0=nrow[:, 0:1],
                                   scalar1=-1.0 / D, scalar2=None, op0=ALU.mult)

        def f_n2(sc, i):
            tc = i
            return sc.activation(out=ynt[:, 0:D], in_=yst[:, tc * D:(tc + 1) * D],
                                 func=AF.Identity, bias=nrow[:, 1:2], scale=1.0)

        def f_n3(v, i):
            v.tensor_mul(out=ynt[:, D:2 * D], in0=ynt[:, 0:D], in1=ynt[:, 0:D])
            v.drain()
            v.reduce_sum(out=nrow[:, 2:3], in_=ynt[:, D:2 * D], axis=AX)
            v.drain()
            return v.tensor_scalar(out=nrow[:, 2:3], in0=nrow[:, 2:3],
                                   scalar1=1.0 / D, scalar2=None, op0=ALU.mult)

        def f_n4(sc, i):
            sc.activation(out=nrow[:, 3:4], in_=nrow[:, 2:3], func=AF.Ln,
                          bias=c.eps[:, 0:1], scale=1.0)
            sc.drain()
            sc.activation(out=nrow[:, 3:4], in_=nrow[:, 3:4], func=AF.Exp, scale=-0.5)
            sc.drain()
            return sc.activation(out=ynt[:, 0:D], in_=ynt[:, 0:D], func=AF.Copy,
                                 scale=nrow[:, 3:4])

        def f_n5(v, i):
            v.tensor_mul(out=ynt[:, 0:D], in0=ynt[:, 0:D], in1=gbt[:])
            v.drain()
            return v.tensor_add(out=ynt[:, 0:D], in0=ynt[:, 0:D], in1=bbt[:])

        def f_n6(te, i):
            last = None
            for dc in range(DC):
                last = te.transpose(c.pbt[:, dc * 128:(dc + 1) * 128],
                                    ynt[:, dc * 128:(dc + 1) * 128], c.ident[:])
            return last

        def f_n7(v, i):
            tc = i
            tb = (tc // 4) * DC * TT
            last = None
            for dc in range(DC):
                last = v.tensor_copy(
                    out=yfm[:, tb + dc * TT + (tc % 4) * 128:
                            tb + dc * TT + (tc % 4) * 128 + 128],
                    in_=c.pbt[:, dc * 128:(dc + 1) * 128])
            return last

        nst = [
            Stage("n1", "vector", f_n1, deps=[("n2", lambda i: i)], ctr=P("n1"),
                  pre=[(gb, GB)]),
            Stage("n2", "scalar", f_n2, deps=[("n1", one), ("n3", lambda i: i)],
                  ctr=P("n2")),
            Stage("n3", "vector", f_n3, deps=[("n2", one)], ctr=P("n3")),
            Stage("n4", "scalar", f_n4, deps=[("n3", one), ("n5", lambda i: i)],
                  ctr=P("n4")),
            Stage("n5", "vector", f_n5, deps=[("n4", one), ("n6", lambda i: i)],
                  ctr=P("n5")),
            Stage("n6", "tensor", f_n6, deps=[("n5", one), ("n7", lambda i: i)],
                  ctr=P("n6"), pre=[c.consts_ready]),
            Stage("n7", "vector", f_n7, deps=[("n6", one)], ctr=P("n7")),
        ]
        run_pipeline(nc, nst, NTC, f"g{half}n")

        NW = HTILES * DC

        def f_wold(g, i):
            tt, dout = divmod(i, DC)
            return g.dma_start(
                out=woring[:, (dout % 2) * DC * 128:((dout % 2) + 1) * DC * 128]
                    .rearrange("p (kc m) -> p kc m", kc=DC),
                in_=gi("woT").rearrange("(kc p) m -> p kc m", p=128)
                    [:, :, dout * 128:(dout + 1) * 128])

        def f_xld(sy, i, _h=half):
            tt = i // DC
            return sy.dma_start(
                out=xt[:, (tt % 2) * DC * TT:((tt % 2) + 1) * DC * TT]
                    .rearrange("p (c t) -> p c t", c=DC),
                in_=xin_tile(c, _h * HTILES + tt))

        def f_womm(te, i):
            tt, dout = divmod(i, DC)
            bank = c.pb[4 + dout % 2]
            tb = tt * DC * TT
            last = None
            for kc in range(DC):
                last = te.matmul(
                    bank[:],
                    woring[:, ((dout % 2) * DC + kc) * 128:((dout % 2) * DC + kc + 1) * 128],
                    yfm[:, tb + kc * TT: tb + (kc + 1) * TT],
                    start=(kc == 0), stop=(kc == DC - 1))
            return last

        def f_woev(sc, i):
            tt, dout = divmod(i, DC)
            return sc.activation(out=ynt[:, (dout % 2) * TT:(dout % 2 + 1) * TT],
                                 in_=c.pb[4 + dout % 2][:], func=AF.Copy)

        def f_woadd(g, i):
            tt, dout = divmod(i, DC)
            dst = xt[:, ((tt % 2) * DC + dout) * TT:((tt % 2) * DC + dout + 1) * TT]
            return g.tensor_add(out=dst, in0=dst,
                                in1=ynt[:, (dout % 2) * TT:(dout % 2 + 1) * TT])

        def f_gst(sy, i, _h=half):
            tt = i // DC
            return sy.dma_start(
                out=xout_tile(c, _h * HTILES + tt),
                in_=xt[:, (tt % 2) * DC * TT:((tt % 2) + 1) * DC * TT]
                    .rearrange("p (c t) -> p c t", c=DC))

        wst = [
            Stage("wold", "gpsimd", f_wold, inc=16,
                  deps=[("womm", lambda i: i - 1)], ctr=P("wold")),
            Stage("xld", "sync", f_xld, inc=16, active=lambda i: i % DC == 0,
                  deps=[("gst", lambda i: i // DC - 1)], ctr=P("xld")),
            Stage("womm", "tensor", f_womm,
                  deps=[("wold", lambda i: i + 1), ("woev", lambda i: i - 1)],
                  ctr=P("womm")),
            Stage("woev", "scalar", f_woev,
                  deps=[("womm", lambda i: i + 1), ("woadd", lambda i: i - 1)],
                  ctr=P("woev")),
            Stage("woadd", "gpsimd", f_woadd,
                  deps=[("woev", lambda i: i + 1), ("xld", lambda i: i // DC + 1)],
                  ctr=P("woadd")),
            Stage("gst", "sync", f_gst, inc=16,
                  active=lambda i: i % DC == DC - 1,
                  deps=[("woadd", lambda i: i + 1)], ctr=P("gst")),
        ]
        smw = run_pipeline(nc, wst, NW, f"g{half}w")
        store_done = smw["gst"]

    with nc.Block() as blk:
        @blk.sync
        def _(sy):
            sy.wait_ge(store_done.ctr.sem,
                       store_done.base + store_done.total * 16)
    c.h_pre = [(store_done.ctr, store_done.base + store_done.total * 16)]
    return c


def prep_gla(inputs):
    wq = np.ascontiguousarray(np.asarray(inputs["L1_attn_a_Wq"], np.float32).T)
    wk = np.ascontiguousarray(np.asarray(inputs["L1_attn_a_Wk"], np.float32).T)
    wv = np.ascontiguousarray(np.asarray(inputs["L1_attn_a_Wv"], np.float32).T)
    wg = np.ascontiguousarray(np.asarray(inputs["L1_attn_a_Wg"], np.float32).T)
    wo = np.ascontiguousarray(np.asarray(inputs["L1_attn_a_Wo"], np.float32).T)
    tri = np.triu(np.ones((64, 64), np.float32))
    tri128 = np.concatenate([tri, tri], 0)
    tri8 = np.tile(tri128, (1, 8))
    return {"wqT": wq, "wkT": wk, "wvT": wv, "wgT": wg, "woT": wo,
            "gamma_row": np.asarray(inputs["L1_attn_a_gamma"], np.float32).reshape(1, D),
            "beta_row": np.asarray(inputs["L1_attn_a_beta"], np.float32).reshape(1, D),
            "c_triu8": tri8}


def _rope_tables():
    hd = HD
    inv_freq = 1.0 / 10000.0 ** (np.arange(0, hd, 2, dtype=np.float32) / hd)
    t = np.arange(SEQ, dtype=np.float32)
    freqs = t[:, None] * inv_freq[None, :]
    emb = np.concatenate([np.sin(freqs), np.cos(freqs)], axis=-1)
    sinp = emb[:, 0::2]
    cosp = emb[:, 1::2]
    ca = np.empty((128, SEQ), np.float32)
    sb = np.empty((128, SEQ), np.float32)
    for g in range(4):
        ca[g * 32:(g + 1) * 32] = cosp.T
        sb[g * 32:(g + 1) * 32] = (-sinp.T) if (g % 2 == 0) else sinp.T
    return ca, sb


def _perm_eo():
    p = np.concatenate([np.arange(0, 64, 2), np.arange(1, 64, 2)])
    full = np.concatenate([h * 64 + p for h in range(H)])
    return full


def prep_mla(inputs):
    wkvd = np.ascontiguousarray(np.asarray(inputs["L0_attn_b_w_kv_down"], np.float32))
    wkvu = np.ascontiguousarray(np.asarray(inputs["L0_attn_b_w_kv_up"], np.float32))
    wq = np.ascontiguousarray(np.asarray(inputs["L0_attn_b_w_q"], np.float32))
    wo = np.ascontiguousarray(np.asarray(inputs["L0_attn_b_w_out"], np.float32))
    tril = np.triu(np.ones((128, 128), np.float32))
    return {"w_kv_down": wkvd, "w_kv_up": wkvu, "w_q": wq, "w_out": wo,
            "c_tril": tril}


def prep_rope(inputs):
    perm = _perm_eo()
    qkv_w = np.asarray(inputs["L1_attn_b_qkv_w"], np.float32)
    qkv_b = np.asarray(inputs["L1_attn_b_qkv_b"], np.float32)
    wq = qkv_w[0:D][perm]
    wk = qkv_w[D:2 * D][perm]
    wv = qkv_w[2 * D:3 * D]
    qkv_wT = np.ascontiguousarray(np.concatenate([wq, wk, wv], 0).T)
    bq = qkv_b[0:D][perm]
    bk = qkv_b[D:2 * D][perm]
    bv = qkv_b[2 * D:3 * D]
    qkb = np.stack([bq.reshape(DC, 128), bk.reshape(DC, 128)], 0)
    qkb = np.ascontiguousarray(qkb.reshape(16, 128).T)
    qkb = np.concatenate([qkb, np.zeros((128, 8), np.float32)], 1)
    out_wT = np.ascontiguousarray(np.asarray(inputs["L1_attn_b_out_w"], np.float32).T)
    out_b = np.ascontiguousarray(
        np.asarray(inputs["L1_attn_b_out_b"], np.float32).reshape(DC, 128).T)
    ca, sb = _rope_tables()
    return {"qkv_wT": qkv_wT, "qkv_b": qkb, "out_wT": out_wT, "out_b": out_b,
            "vb_row": bv.reshape(1, D).astype(np.float32),
            "rope_ca": ca, "rope_sb": sb}


def run_sublayer(c, wmap, xshards):
    in_maps = []
    consts = host_consts()
    for cid in range(NCORES):
        m = {"xin": np.ascontiguousarray(xshards[cid])}
        m.update(wmap)
        m.update(consts)
        in_maps.append(m)
    res = run_bass_kernel_spmd(c.nc, in_maps, core_ids=list(range(NCORES)))
    return [r["xout"].copy() for r in res.results]


FUSED_CHAIN = [("lc_", "local"), ("f0_", "l0ffn"), ("ml_", "mla"),
               ("f0_", "l0ffn"), ("gl_", "gla"), ("mo_", "moe"),
               ("rp_", "rope"), ("mo_", "moe")]
_EXTRAS = {"lc_": None, "f0_": None, "ml_": None, "gl_": None, "mo_": None,
           "rp_": None}


def build_fused_program():
    extras = {"lc_": LOCAL_EXTRA, "f0_": L0FFN_EXTRA, "ml_": ATTN_EXTRA["mla"],
              "gl_": GLA_EXTRA, "mo_": MOE_EXTRA, "rp_": ATTN_EXTRA["rope"]}
    allx = []
    for pfx, ex in extras.items():
        allx += [(pfx + nm, shape, dt) for nm, shape, dt in ex]
    c = new_program("fused", allx)
    nc = c.nc
    scratch = [nc.dram_tensor(f"xs_{j}", [DC, 128, T], F32, kind="Internal").ap()
               for j in range(2)]
    emitters = {
        "local": lambda gi, pfx: emit_local(c, gi, pfx),
        "l0ffn": lambda gi, pfx: emit_l0ffn(c, gi, pfx),
        "mla": lambda gi, pfx: emit_attn(c, gi, "mla", pfx),
        "gla": lambda gi, pfx: emit_gla(c, gi, pfx),
        "moe": lambda gi, pfx: emit_moe(c, gi, pfx),
        "rope": lambda gi, pfx: emit_attn(c, gi, "rope", pfx),
    }
    n = len(FUSED_CHAIN)
    for k, (pfx, kind) in enumerate(FUSED_CHAIN):
        c.cur_xin = c.xin if k == 0 else scratch[k % 2]
        c.cur_xout = c.xout if k == n - 1 else scratch[(k + 1) % 2]
        aa_reset(c)
        emitters[kind]((lambda _p: (lambda nm: c.inputs[_p + nm]))(pfx), pfx)
    return c


_PROG_CACHE = {}


def _get_prog(kind):
    if kind not in _PROG_CACHE:
        _PROG_CACHE[kind] = {
            "local": build_local_program,
            "l0ffn": build_l0ffn_program,
            "mla": lambda: build_attn_program("mla"),
            "gla": build_gla_program,
            "moe": build_moe_program,
            "rope": lambda: build_attn_program("rope"),
            "fused": build_fused_program,
        }[kind]()
    return _PROG_CACHE[kind]


def kernel(**inputs):
    xshards = shard_x(np.ascontiguousarray(np.asarray(
        inputs["L0_attn_a_hidden_states"], dtype=np.float32)))
    consts = host_consts()
    preps = {"lc_": prep_local, "f0_": prep_l0ffn, "ml_": prep_mla,
             "gl_": prep_gla, "mo_": prep_moe, "rp_": prep_rope}
    wmap = {}
    for pfx, prep in preps.items():
        for nm, arr in prep(inputs).items():
            wmap[pfx + nm] = arr
    c = _get_prog("fused")
    in_maps = []
    for cid in range(NCORES):
        m = {"xin": np.ascontiguousarray(xshards[cid])}
        m.update(wmap)
        m.update(consts)
        in_maps.append(m)
    res = run_bass_kernel_spmd(c.nc, in_maps, core_ids=list(range(NCORES)))
    return unshard_x([np.asarray(r["xout"]) for r in res.results]).astype(np.float32)
